# version 02
# baseline (speedup 1.0000x reference)
import contextlib
import numpy as np
import concourse.bass as bass
import concourse.mybir as mybir
from concourse.bass_utils import run_bass_kernel_spmd

F32 = mybir.dt.float32
BF16 = mybir.dt.bfloat16
I8 = mybir.dt.int8
AF = mybir.ActivationFunctionType
ALU = mybir.AluOpType
AX = mybir.AxisListType

TPC = 1024
NCH = 8
H = 16
DM = 2048
KT = 16
NIT = 64
EPS_LN = 1e-5
EPS_Z = 1e-6
VST = 130

ENGINES = ("pe", "act", "dve", "gp", "sy")
COMPUTE_SEMS = {"pe", "act", "dve"}


class Sched:

    def __init__(self):
        self.recs = {e: [] for e in ENGINES}
        self.cnt = {}
        self.marks = {}

    def checkpoint(self, name):
        self.marks[name] = ({e: len(self.recs[e]) for e in ENGINES},
                            dict(self.cnt))

    def emit(self, eng, fn, waits=(), inc=None, amt=1):
        waits = [w for w in waits if w is not None]
        waits = [(s, v) for (s, v) in waits if v > 0]
        mark = None
        if inc is not None:
            self.cnt[inc] = self.cnt.get(inc, 0) + (amt if amt else 1)
            mark = (inc, self.cnt[inc])
        self.recs[eng].append((fn, waits, inc if mark else None, amt))
        return mark

    def val(self, sem):
        return self.cnt.get(sem, 0)


def build_program(dumps=(), stop_after=None):
    nc = bass.Bass("TRN2", target_bir_lowering=False)

    xc = nc.dram_tensor("xc", [TPC, DM], F32, kind="ExternalInput").ap()
    w4 = {p: nc.dram_tensor(f"w4{p}", [8, KT, 128, 256], F32, kind="ExternalInput").ap()
          for p in "qkvo"}
    wg5 = nc.dram_tensor("wg5", [32, 4, 128, 4, 256], I8, kind="ExternalInput").ap()
    wu5 = nc.dram_tensor("wu5", [32, 4, 128, 4, 256], I8, kind="ExternalInput").ap()
    wd5 = nc.dram_tensor("wd5", [4, 32, 128, 2, 512], I8, kind="ExternalInput").ap()
    bqkv_d = nc.dram_tensor("bqkv", [128, 48], F32, kind="ExternalInput").ap()
    bo_bc_d = nc.dram_tensor("bo_bc", [128, DM], F32, kind="ExternalInput").ap()
    sdown_d = nc.dram_tensor("sdown_bc", [128, DM], F32, kind="ExternalInput").ap()
    sg_d = nc.dram_tensor("sg_t", [128, NIT], F32, kind="ExternalInput").ap()
    su_d = nc.dram_tensor("su_t", [128, NIT], F32, kind="ExternalInput").ap()
    pm_d = nc.dram_tensor("pmask", [128, 4], F32, kind="ExternalInput").ap()
    id_d = nc.dram_tensor("id128", [128, 128], F32, kind="ExternalInput").ap()
    tri_d = nc.dram_tensor("triu", [128, 128], F32, kind="ExternalInput").ap()

    yout = nc.dram_tensor("yout", [TPC, DM], F32, kind="ExternalOutput").ap()
    dump_specs = {
        "d_lnx": ([128, KT * TPC], BF16), "d_kftm": ([128, NCH * H * 128], BF16),
        "d_vtm": ([128, NCH * H * VST], BF16), "d_kvtot": ([128, H * VST], F32),
        "d_state0": ([128, H * VST], F32), "d_qf": ([128, H * TPC], BF16),
        "d_yscr": ([TPC, DM], F32), "d_ln2x": ([128, KT * TPC], BF16),
        "d_xs1": ([128, NCH * DM], F32),
        "d_xa": ([128, DM], F32), "d_stsc": ([128, 8], F32),
        "d_fuse": ([128, NIT * 256], BF16), "d_z": ([128, 2 * DM], F32),
        "d_lntm": ([128, DM], BF16), "d_sp": ([128, 32], F32),
    }
    dump_t = {n: nc.dram_tensor(n, s, d, kind="ExternalOutput").ap()
              for n, (s, d) in dump_specs.items() if n in dumps}

    y_scr = nc.dram_tensor("y_scr", [TPC, DM], F32, kind="Internal").ap()
    cc_in = nc.dram_tensor("cc_in", [128, H * VST], F32, kind="Internal")
    cc_out = nc.dram_tensor("cc_out", [4 * 128, H * VST], F32, kind="Internal")

    S = Sched()
    stack = contextlib.ExitStack()

    def sb(name, shape, dtype):
        return stack.enter_context(nc.sbuf_tensor(name, shape, dtype))

    lnx_fm = sb("lnx_fm", [128, KT * TPC], BF16)
    ident = sb("ident", [128, 128], BF16)
    triu = sb("triu_s", [128, 128], BF16)
    bqkv = sb("bqkv_s", [128, 48], F32)
    bo_bc = sb("bo_bc_s", [128, DM], BF16)
    sg_t = sb("sg_s", [128, NIT], F32)
    su_t = sb("su_s", [128, NIT], F32)
    pmask = sb("pmask_s", [128, 4], F32)
    kv_state = sb("kv_state", [128, H * VST], F32)
    state_bf = sb("state_bf", [128, H * VST], BF16)
    mb_arr = sb("mb_arr", [128, NCH], F32)
    rb_arr = sb("rb_arr", [128, NCH], F32)
    nb_arr = sb("nb_arr", [128, NCH], F32)
    sp1 = sb("sp1", [128, 32], F32)
    sp2 = sb("sp2", [128, 32], F32)
    st_sc = sb("st_sc", [128, 8], F32)
    rz_sc = sb("rz_sc", [128, 2], F32)
    junk_s = sb("junk_s", [128, 512], BF16)
    junk_a = sb("junk_a", [128, 512], BF16)

    GP_DMA_SEMS = ["cg", "w0", "w1", "wg0", "wg1", "wu0", "wu1",
                   "wd0", "wd1", "sd5", "gdmp"]
    SY_DMA_SEMS = ["cs", "xa", "dmp", "dmp2", "ccin", "gath", "ysl0", "ysl1",
                   "x5", "y5", "out"]
    sem_names = ["pe", "act", "dve", "cc"] + GP_DMA_SEMS + SY_DMA_SEMS
    sems = {n: stack.enter_context(nc.semaphore(f"s_{n}")) for n in sem_names}

    def gdma(sem, fn, waits=()):
        return S.emit("gp", fn, waits=waits, inc=sem, amt=16)

    def sdma(sem, fn, waits=()):
        return S.emit("sy", fn, waits=waits, inc=sem, amt=16)

    def pe(fn, waits=(), inc=True):
        return S.emit("pe", fn, waits=waits, inc="pe" if inc else None)

    def act(fn, waits=(), inc=True):
        return S.emit("act", fn, waits=waits, inc="act")

    def dve(fn, waits=(), inc=True):
        return S.emit("dve", fn, waits=waits, inc="dve")

    att_stack = contextlib.ExitStack()

    def sba(name, shape, dtype):
        return att_stack.enter_context(nc.sbuf_tensor(name, shape, dtype))

    qf_fm = sba("qf_fm", [128, H * TPC], BF16)
    kf_tm = sba("kf_tm", [128, NCH * H * 128], BF16)
    v_tm = sba("v_tm", [128, NCH * H * VST], BF16)
    out_fm = sba("out_fm", [128, H * 4 * 128], BF16)
    kv_total = sba("kv_total", [128, H * VST], F32)
    wbuf = [sba(f"wbuf{i}", [128, KT * 256], BF16) for i in range(2)]
    r_st = [sba(f"r_st{i}", [128, 512], BF16) for i in range(2)]
    fm_st = [sba(f"fm_st{i}", [128, 512], BF16) for i in range(2)]
    kfT_st = [sba(f"kfT{i}", [128, 128], BF16) for i in range(2)]
    AT_st = [sba(f"AT{i}", [128, 128], BF16) for i in range(2)]
    xa_p1 = sba("xa_p1", [128, DM], F32)
    lntm_st = sba("lntm_st", [128, DM], BF16)
    ysl_st = [sba(f"ysl{i}", [128, 256], F32) for i in range(2)]

    ps_a = stack.enter_context(nc.psum_tensor("ps_a", [128, 512], F32))
    ps_b = stack.enter_context(nc.psum_tensor("ps_b", [128, 512], F32))
    ps_c = stack.enter_context(nc.psum_tensor("ps_c", [128, 512], F32))
    ps_d = stack.enter_context(nc.psum_tensor("ps_d", [128, 512], F32))
    ps_t1 = stack.enter_context(nc.psum_tensor("ps_t1", [128, 512], BF16))
    ps_t2 = stack.enter_context(nc.psum_tensor("ps_t2", [128, 512], BF16))
    ps_s1 = stack.enter_context(nc.psum_tensor("ps_s1", [128, VST], F32))
    ps_s2 = stack.enter_context(nc.psum_tensor("ps_s2", [128, VST], F32))

    gdma("cg", lambda g: g.dma_start(out=ident[:, :], in_=id_d[:, :]))
    gdma("cg", lambda g: g.dma_start(out=triu[:, :], in_=tri_d[:, :]))
    gdma("cg", lambda g: g.dma_start(out=bo_bc[:, :], in_=bo_bc_d[:, :]))
    consts_g = ("cg", S.val("cg"))
    sdma("cs", lambda s: s.dma_start(out=bqkv[:, :], in_=bqkv_d[:, :]))
    sdma("cs", lambda s: s.dma_start(out=sg_t[:, :], in_=sg_d[:, :]))
    sdma("cs", lambda s: s.dma_start(out=su_t[:, :], in_=su_d[:, :]))
    sdma("cs", lambda s: s.dma_start(out=pmask[:, :], in_=pm_d[:, :]))
    consts_s = ("cs", S.val("cs"))
    ones_m = dve(lambda v: v.memset(
        v_tm.ap().rearrange("p (nh s) -> p nh s", s=VST)[:, :, 128:VST], 1.0))
    kvt_z = dve(lambda v: v.memset(kv_total[:, :], 0.0))


    def ln_stats(src_full, waits_src):
        a1 = d1 = None
        for q in range(4):
            sl = src_full[:, q * 512:(q + 1) * 512]
            a1 = act(lambda s, sl=sl, q=q: s.activation(
                junk_a[:, 0:512], sl, AF.Identity, bias=0.0, scale=1.0,
                accum_out=sp1[:, 24 + q:25 + q]),
                waits=waits_src if q == 0 else [])
            d1 = dve(lambda v, sl=sl, q=q: v.scalar_tensor_tensor(
                junk_s[:, 0:512], sl, 1.0, sl, ALU.mult, ALU.mult,
                accum_out=sp2[:, 24 + q:25 + q]),
                waits=waits_src if q == 0 else [])
        dve(lambda v: v.reduce_sum(st_sc[:, 0:1], sp1[:, 24:28], axis=AX.X),
            waits=[a1, d1])
        dve(lambda v: v.reduce_sum(st_sc[:, 1:2], sp2[:, 24:28], axis=AX.X))
        dve(lambda v: v.tensor_scalar_mul(st_sc[:, 2:3], st_sc[:, 0:1], 1.0 / DM))
        dve(lambda v: v.tensor_mul(st_sc[:, 6:7], st_sc[:, 2:3], st_sc[:, 2:3]))
        d4 = dve(lambda v: v.scalar_tensor_tensor(
            st_sc[:, 3:4], st_sc[:, 1:2], 1.0 / DM, st_sc[:, 6:7],
            ALU.mult, ALU.subtract))
        d4 = dve(lambda v: v.tensor_scalar_add(st_sc[:, 3:4], st_sc[:, 3:4],
                                               EPS_LN))
        act(lambda s: s.activation(st_sc[:, 7:8], st_sc[:, 3:4], AF.Ln,
                                   bias=0.0, scale=1.0), waits=[d4])
        a3 = act(lambda s: s.activation(st_sc[:, 4:5], st_sc[:, 7:8], AF.Exp,
                                        bias=0.0, scale=-0.5))
        d5 = dve(lambda v: v.scalar_tensor_tensor(
            st_sc[:, 5:6], st_sc[:, 2:3], -1.0, st_sc[:, 4:5],
            ALU.mult, ALU.mult), waits=[a3])
        return d5

    tpose_ps = [ps_t1, ps_t2]
    tpose_last = [None, None]
    tp_rot = [0]

    def transpose4(src_ap_fn, dst_list, waits_src):
        pi = tp_rot[0] % 2
        tp_rot[0] += 1
        psT = tpose_ps[pi]
        waits = list(waits_src) + [consts_g]
        if tpose_last[pi] is not None:
            waits.append(tpose_last[pi])
        tr = None
        for j in range(4):
            tr = pe(lambda t, j=j, psT=psT: t.transpose(
                psT[:, j * 128:(j + 1) * 128], src_ap_fn(j), ident[:, :]),
                waits=waits if j == 0 else [])
        ev = None
        for j in range(4):
            buf, col = dst_list[j]
            ev = act(lambda s, j=j, psT=psT, buf=buf, col=col: s.activation(
                buf[:, col:col + 128], psT[:, j * 128:(j + 1) * 128],
                AF.Identity, bias=0.0, scale=1.0), waits=[tr] if j == 0 else [])
        tpose_last[pi] = ev
        return ev

    p1_last = None
    prev_consumed = None
    for tc in range(NCH):
        ld = sdma("xa", lambda s, tc=tc: s.dma_start(
            out=xa_p1[:, :], in_=xc[tc * 128:(tc + 1) * 128, :]),
            waits=[prev_consumed] if prev_consumed else [])
        stt = ln_stats(xa_p1, [ld])
        if tc == 0 and "d_xa" in dump_t:
            sdma("dmp", lambda s: s.dma_start(out=dump_t["d_xa"][:, :], in_=xa_p1[:, :]),
                 waits=[stt])
        if tc == 0 and "d_stsc" in dump_t:
            sdma("dmp", lambda s: s.dma_start(out=dump_t["d_stsc"][:, :], in_=st_sc[:, :]),
                 waits=[stt])
        if tc == 0 and "d_sp" in dump_t:
            sdma("dmp", lambda s: s.dma_start(out=dump_t["d_sp"][:, :], in_=sp1[:, :]),
                 waits=[stt])
        ap = act(lambda s: s.activation(
            lntm_st[:, :], xa_p1[:, :], AF.Identity,
            bias=st_sc[:, 5:6], scale=st_sc[:, 4:5]), waits=[stt])
        if tc == 0 and "d_lntm" in dump_t:
            sdma("dmp", lambda s: s.dma_start(out=dump_t["d_lntm"][:, :], in_=lntm_st[:, :]),
                 waits=[ap])
        prev_consumed = ap
        for b4 in range(4):
            p1_last = transpose4(
                lambda j, b4=b4: lntm_st[:, (b4 * 4 + j) * 128:(b4 * 4 + j + 1) * 128],
                [(lnx_fm, (b4 * 4 + j) * TPC + tc * 128) for j in range(4)],
                [ap])
    lnx_ready = p1_last

    if "d_lnx" in dump_t:
        sdma("dmp", lambda s: s.dma_start(out=dump_t["d_lnx"][:, :], in_=lnx_fm[:, :]),
             waits=[lnx_ready])
    S.checkpoint("p1")

    wlast_use = [None, None]
    proj_ps = [ps_a, ps_b]
    proj_ps_last = [None, None]

    def stream_attn_w(p, ob, bi):
        waits = []
        if wlast_use[bi] is not None:
            waits.append(wlast_use[bi])
        return gdma(f"w{bi}", lambda g, p=p, ob=ob, bi=bi: g.dma_start(
            out=wbuf[bi].ap().rearrange("p (kt o) -> p kt o", kt=KT),
            in_=w4[p][ob].rearrange("kt f o -> f kt o")), waits=waits)

    elu_n = [0]

    def elu_evac(ps, mm, bcol, st_i):
        first = elu_n[0] == 0
        elu_n[0] += 1
        a1 = act(lambda s: s.activation(
            r_st[st_i][:, :], ps[:, 0:512], AF.Relu,
            bias=bqkv[:, bcol:bcol + 1], scale=1.0), waits=[mm, consts_s])
        if first:
            S.checkpoint("e1")
        d1 = dve(lambda v: v.tensor_scalar(
            fm_st[st_i][:, :], ps[:, 0:512], bqkv[:, bcol:bcol + 1], 0.0,
            ALU.add, ALU.min), waits=[mm, consts_s, a1])
        if first:
            S.checkpoint("e2")
        a2 = act(lambda s: s.activation(
            fm_st[st_i][:, :], fm_st[st_i][:, :], AF.Exp, bias=0.0, scale=1.0),
            waits=[d1])
        if first:
            S.checkpoint("e3")
        d2 = dve(lambda v: v.tensor_add(
            fm_st[st_i][:, :], r_st[st_i][:, :], fm_st[st_i][:, :]),
            waits=[a1, a2])
        if first:
            S.checkpoint("e4")
        return d2

    def proj_pass(p, consume):
        for ob in range(8):
            if p == "k" and ob == 1:
                S.checkpoint("p2k0")
            bi = ob % 2
            wld = stream_attn_w(p, ob, bi)
            for th in range(2):
                for osub in range(2):
                    h = ob * 2 + osub
                    k = (th * 2 + osub) % 2
                    ps = proj_ps[k]
                    waits = [wld, lnx_ready]
                    if proj_ps_last[k] is not None:
                        waits.append(proj_ps_last[k])
                    mm = None
                    for kt in range(KT):
                        mm = pe(lambda t, ps=ps, kt=kt, bi=bi, osub=osub, th=th:
                                t.matmul(
                                    ps[:, 0:512],
                                    wbuf[bi][:, kt * 256 + osub * 128:
                                             kt * 256 + (osub + 1) * 128],
                                    lnx_fm[:, kt * TPC + th * 512:
                                           kt * TPC + (th + 1) * 512],
                                    start=(kt == 0), stop=(kt == KT - 1)),
                                waits=waits if kt == 0 else [],
                                inc=(kt == KT - 1))
                    wlast_use[bi] = mm
                    if p == "k" and ob == 0 and th == 0 and osub == 0:
                        S.checkpoint("p2m")
                    proj_ps_last[k] = consume(h, th, ps, mm, k)
                    if p == "k" and ob == 0 and th == 0 and osub == 0:
                        S.checkpoint("p2u")

    def consume_k(h, th, ps, mm, st_i):
        rdy = elu_evac(ps, mm, 16 + h, st_i)
        transpose4(
            lambda j: fm_st[st_i][:, j * 128:(j + 1) * 128],
            [(kf_tm, ((th * 4 + j) * H + h) * 128) for j in range(4)],
            [rdy])
        return rdy

    def consume_v(h, th, ps, mm, st_i):
        rdy = act(lambda s: s.activation(
            fm_st[st_i][:, :], ps[:, 0:512], AF.Identity,
            bias=bqkv[:, 32 + h:32 + h + 1], scale=1.0), waits=[mm, consts_s])
        transpose4(
            lambda j: fm_st[st_i][:, j * 128:(j + 1) * 128],
            [(v_tm, ((th * 4 + j) * H + h) * VST) for j in range(4)],
            [rdy])
        return rdy

    proj_pass("k", consume_k)
    S.checkpoint("p2k")
    proj_pass("v", consume_v)
    p2_done_act = ("act", S.val("act"))

    if "d_kftm" in dump_t:
        sdma("dmp", lambda s: s.dma_start(out=dump_t["d_kftm"][:, :], in_=kf_tm[:, :]),
             waits=[p2_done_act])
    if "d_vtm" in dump_t:
        sdma("dmp", lambda s: s.dma_start(out=dump_t["d_vtm"][:, :], in_=v_tm[:, :]),
             waits=[p2_done_act, ones_m])
    S.checkpoint("p2")

    tot_ps = [ps_s1, ps_s2]
    tot_last = [None, None]
    tot_ev = None
    for h in range(H):
        k = h % 2
        kps = tot_ps[k]
        waits = [p2_done_act, ones_m]
        if tot_last[k] is not None:
            waits.append(tot_last[k])
        mm = None
        for n in range(NCH):
            mm = pe(lambda t, kps=kps, n=n, h=h: t.matmul(
                kps[:, 0:129],
                kf_tm[:, (n * H + h) * 128:(n * H + h) * 128 + 128],
                v_tm[:, (n * H + h) * VST:(n * H + h) * VST + 129],
                start=(n == 0), stop=(n == NCH - 1)),
                waits=waits if n == 0 else [], inc=(n == NCH - 1))
        tot_ev = dve(lambda v, kps=kps, h=h: v.tensor_copy(
            kv_total[:, h * VST:h * VST + 129], kps[:, 0:129]), waits=[mm])
        tot_last[k] = tot_ev
    totals_done = tot_ev
    if "d_kvtot" in dump_t:
        sdma("dmp", lambda s: s.dma_start(out=dump_t["d_kvtot"][:, :], in_=kv_total[:, :]),
             waits=[totals_done])

    st_in = sdma("ccin", lambda s: s.dma_start(out=cc_in.ap()[:, :], in_=kv_total[:, :]),
                 waits=[totals_done])
    S.emit("gp", lambda g: g.collective_compute(
        "AllGather", ALU.bypass,
        replica_groups=[[0, 1, 2, 3], [4, 5, 6, 7]],
        ins=[cc_in.ap().opt()], outs=[cc_out.ap().opt()]),
        waits=[st_in], inc="cc", amt=None)
    cc_done = ("cc", 1)
    S.checkpoint("p2b")

    qf_last = [None]

    def consume_q(h, th, ps, mm, st_i):
        rdy = elu_evac(ps, mm, h, st_i)
        qf_last[0] = dve(lambda v, h=h, th=th, st_i=st_i: v.tensor_copy(
            qf_fm[:, h * TPC + th * 512: h * TPC + (th + 1) * 512],
            fm_st[st_i][:, :]), waits=[rdy])
        return qf_last[0]

    proj_pass("q", consume_q)
    qf_done = qf_last[0]
    p3_done_pe = ("pe", S.val("pe"))

    if "d_qf" in dump_t:
        sdma("dmp", lambda s: s.dma_start(out=dump_t["d_qf"][:, :], in_=qf_fm[:, :]),
             waits=[qf_done])
    S.checkpoint("p3")

    pf = dve(lambda v: v.memset(kv_state[:, :], 0.0), waits=[cc_done, totals_done])
    for r in range(4):
        gl = sdma("gath", lambda s, r=r: s.dma_start(
            out=kv_total[:, :], in_=cc_out.ap()[r * 128:(r + 1) * 128, :]),
            waits=[cc_done] + ([pf] if r > 0 else []))
        pf = dve(lambda v, r=r: v.scalar_tensor_tensor(
            kv_state[:, :], kv_total[:, :], pmask[:, r:r + 1], kv_state[:, :],
            ALU.mult, ALU.add), waits=[gl, consts_s])
    st_f32 = pf
    st_upd = act(lambda s: s.activation(state_bf[:, :], kv_state[:, :],
                                        AF.Identity, bias=0.0, scale=1.0),
                 waits=[pf])
    if "d_state0" in dump_t:
        sdma("dmp", lambda s: s.dma_start(out=dump_t["d_state0"][:, :], in_=kv_state[:, :]),
             waits=[pf])
    S.checkpoint("p3b")

    kfT_last = tpose_last[0]
    A_last = proj_ps_last[0]
    O_last = None
    oT_last = tpose_last[1]
    kv_ps_last = [tot_last[0], tot_last[1]]
    y_ps_last = [proj_ps_last[1], None]
    yps = [ps_b, ps_d]
    ysl_dma = [None, None]

    for grp in range(2):
        for n4 in range(4):
            n = grp * 4 + n4
            for h in range(H):
                si = (n * H + h) % 2
                waits = [p2_done_act]
                if kfT_last is not None:
                    waits.append(kfT_last)
                t1 = pe(lambda t, n=n, h=h: t.transpose(
                    ps_t1[:, 0:128],
                    kf_tm[:, (n * H + h) * 128:(n * H + h) * 128 + 128],
                    ident[:, :]), waits=waits + [consts_g])
                e1 = act(lambda s, si=si: s.activation(
                    kfT_st[si][:, :], ps_t1[:, 0:128], AF.Identity,
                    bias=0.0, scale=1.0), waits=[t1])
                kfT_last = e1
                waits = [e1, qf_done]
                if A_last is not None:
                    waits.append(A_last)
                m1 = pe(lambda t, n=n, h=h, si=si: t.matmul(
                    ps_a[:, 0:128], kfT_st[si][:, :],
                    qf_fm[:, h * TPC + n * 128: h * TPC + (n + 1) * 128],
                    start=True, stop=True), waits=waits)
                d1 = dve(lambda v, si=si: v.tensor_mul(
                    AT_st[si][:, :], ps_a[:, 0:128], triu[:, :]),
                    waits=[m1, consts_g])
                A_last = d1
                waits = [d1, st_upd]
                if O_last is not None:
                    waits.append(O_last)
                m2 = pe(lambda t, n=n, h=h, si=si: t.matmul(
                    ps_c[:, 0:129], AT_st[si][:, :],
                    v_tm[:, (n * H + h) * VST:(n * H + h) * VST + 129],
                    start=True, stop=False), waits=waits)
                m3 = pe(lambda t, n=n, h=h: t.matmul(
                    ps_c[:, 0:129],
                    qf_fm[:, h * TPC + n * 128: h * TPC + (n + 1) * 128],
                    state_bf[:, h * VST:h * VST + 129],
                    start=False, stop=True), inc=True)
                dve(lambda v: v.tensor_scalar_add(
                    rz_sc[:, 0:1], ps_c[:, 128:129], EPS_Z), waits=[m3])
                d3 = dve(lambda v: v.reciprocal(rz_sc[:, 1:2], rz_sc[:, 0:1]))
                d4 = dve(lambda v, si=si: v.tensor_scalar_mul(
                    AT_st[si][:, :], ps_c[:, 0:128], rz_sc[:, 1:2]), waits=[d3])
                O_last = d4
                waits = [d4]
                if oT_last is not None:
                    waits.append(oT_last)
                t2 = pe(lambda t, si=si: t.transpose(
                    ps_t2[:, 0:128], AT_st[si][:, :], ident[:, :]), waits=waits)
                e2 = act(lambda s, n4=n4, h=h: s.activation(
                    out_fm[:, (h * 4 + n4) * 128:(h * 4 + n4) * 128 + 128],
                    ps_t2[:, 0:128], AF.Identity, bias=0.0, scale=1.0),
                    waits=[t2])
                oT_last = e2
                ki = h % 2
                kps = tot_ps[ki]
                waits = []
                if kv_ps_last[ki] is not None:
                    waits.append(kv_ps_last[ki])
                m4 = pe(lambda t, n=n, h=h, kps=kps: t.matmul(
                    kps[:, 0:129],
                    kf_tm[:, (n * H + h) * 128:(n * H + h) * 128 + 128],
                    v_tm[:, (n * H + h) * VST:(n * H + h) * VST + 129],
                    start=True, stop=True), waits=waits)
                d5 = dve(lambda v, h=h, kps=kps: v.tensor_add(
                    kv_state[:, h * VST:h * VST + 129],
                    kv_state[:, h * VST:h * VST + 129], kps[:, 0:129]),
                    waits=[m4, st_upd])
                kv_ps_last[ki] = d5
                st_upd = act(lambda s, h=h: s.activation(
                    state_bf[:, h * VST:h * VST + 129],
                    kv_state[:, h * VST:h * VST + 129],
                    AF.Identity, bias=0.0, scale=1.0), waits=[d5])
        grp_out_ready = oT_last
        for ob in range(8):
            bi = ob % 2
            waits = [grp_out_ready]
            if wlast_use[bi] is not None:
                waits.append(wlast_use[bi])
            wld = gdma(f"w{bi}", lambda g, ob=ob, bi=bi: g.dma_start(
                out=wbuf[bi].ap().rearrange("p (kt o) -> p kt o", kt=KT),
                in_=w4["o"][ob].rearrange("kt f o -> f kt o")), waits=waits)
            for n4 in range(4):
                n = grp * 4 + n4
                yk = n4 % 2
                ps = yps[yk]
                waits = [wld, grp_out_ready, consts_g]
                if y_ps_last[yk] is not None:
                    waits.append(y_ps_last[yk])
                mm = None
                for kt in range(KT):
                    mm = pe(lambda t, ps=ps, kt=kt, n4=n4, bi=bi: t.matmul(
                        ps[:, 0:256],
                        out_fm[:, (kt * 4 + n4) * 128:(kt * 4 + n4) * 128 + 128],
                        wbuf[bi][:, kt * 256:(kt + 1) * 256],
                        start=(kt == 0), stop=(kt == KT - 1)),
                        waits=waits if kt == 0 else [], inc=(kt == KT - 1))
                wlast_use[bi] = mm
                yi = (ob * 4 + n4) % 2
                waits = [mm]
                if ysl_dma[yi] is not None:
                    waits.append(ysl_dma[yi])
                d6 = dve(lambda v, ps=ps, ob=ob, yi=yi: v.tensor_add(
                    ysl_st[yi][:, :], ps[:, 0:256],
                    bo_bc[:, ob * 256:(ob + 1) * 256]), waits=waits)
                y_ps_last[yk] = d6
                a7 = act(lambda s, ob=ob, n4=n4, yi=yi: s.activation(
                    junk_a[:, 0:256], ysl_st[yi][:, :],
                    AF.Identity, bias=0.0, scale=1.0,
                    accum_out=sp1[:, n4 * 8 + ob:n4 * 8 + ob + 1]), waits=[d6])
                d7 = dve(lambda v, ob=ob, n4=n4, yi=yi: v.scalar_tensor_tensor(
                    junk_s[:, 0:256], ysl_st[yi][:, :], 1.0, ysl_st[yi][:, :],
                    ALU.mult, ALU.mult,
                    accum_out=sp2[:, n4 * 8 + ob:n4 * 8 + ob + 1]), waits=[d6])
                ysl_dma[yi] = sdma(f"ysl{yi}", lambda s, n=n, ob=ob, yi=yi: s.dma_start(
                    out=y_scr[n * 128:(n + 1) * 128, ob * 256:(ob + 1) * 256],
                    in_=ysl_st[yi][:, :]), waits=[d7, a7])
        stats_in = [("act", S.val("act")), ("dve", S.val("dve"))]
        for n4 in range(4):
            n = grp * 4 + n4
            dve(lambda v, n4=n4: v.reduce_sum(
                st_sc[:, 0:1], sp1[:, n4 * 8:n4 * 8 + 8], axis=AX.X),
                waits=stats_in)
            dve(lambda v, n4=n4: v.reduce_sum(
                st_sc[:, 1:2], sp2[:, n4 * 8:n4 * 8 + 8], axis=AX.X))
            dve(lambda v, n=n: v.tensor_scalar_mul(
                mb_arr[:, n:n + 1], st_sc[:, 0:1], 1.0 / DM))
            dve(lambda v, n=n: v.tensor_mul(
                st_sc[:, 6:7], mb_arr[:, n:n + 1], mb_arr[:, n:n + 1]))
            d4v = dve(lambda v: v.scalar_tensor_tensor(
                st_sc[:, 3:4], st_sc[:, 1:2], 1.0 / DM, st_sc[:, 6:7],
                ALU.mult, ALU.subtract))
            d4v = dve(lambda v: v.tensor_scalar_add(st_sc[:, 3:4], st_sc[:, 3:4],
                                                    EPS_LN))
            act(lambda s: s.activation(st_sc[:, 7:8], st_sc[:, 3:4], AF.Ln,
                                       bias=0.0, scale=1.0), waits=[d4v])
            a3 = act(lambda s, n=n: s.activation(
                rb_arr[:, n:n + 1], st_sc[:, 7:8], AF.Exp, bias=0.0, scale=-0.5))
            dve(lambda v, n=n: v.scalar_tensor_tensor(
                nb_arr[:, n:n + 1], mb_arr[:, n:n + 1], -1.0, rb_arr[:, n:n + 1],
                ALU.mult, ALU.mult), waits=[a3])
    lnb_done = dve(lambda v: v.engine_nop())
    yspill0 = ("ysl0", S.val("ysl0"))
    yspill1 = ("ysl1", S.val("ysl1"))
    p4_pe = ("pe", S.val("pe"))

    if "d_yscr" in dump_t:
        gdma("gdmp", lambda g: g.dma_start(out=dump_t["d_yscr"][:, :], in_=y_scr[:, :]),
             waits=[yspill0, yspill1])
    S.checkpoint("p4")

    att_stack.close()
    xs = sb("xs", [128, NCH * DM], F32)

    p5a_stack = contextlib.ExitStack()

    def sb5(name, shape, dtype):
        return p5a_stack.enter_context(nc.sbuf_tensor(name, shape, dtype))

    xa5 = sb5("xa5", [128, DM], F32)
    ya5 = sb5("ya5", [128, DM], F32)
    lntm5 = sb5("lntm5", [128, DM], BF16)

    ln2_tile_done = [None] * NCH
    prev_x = None
    prev_y = None
    for tc in range(NCH):
        xld = sdma("x5", lambda s, tc=tc: s.dma_start(
            out=xa5[:, :], in_=xc[tc * 128:(tc + 1) * 128, :]),
            waits=[p4_pe] + ([prev_x] if prev_x else []))
        yld = sdma("y5", lambda s, tc=tc: s.dma_start(
            out=ya5[:, :], in_=y_scr[tc * 128:(tc + 1) * 128, :]),
            waits=[yspill0, yspill1] + ([prev_y] if prev_y else []))
        ayn = act(lambda s, tc=tc: s.activation(
            ya5[:, :], ya5[:, :], AF.Identity,
            bias=nb_arr[:, tc:tc + 1], scale=rb_arr[:, tc:tc + 1]),
            waits=[yld, lnb_done])
        prev_y = ayn
        dx1 = dve(lambda v, tc=tc: v.tensor_add(
            xs[:, tc * DM:(tc + 1) * DM], xa5[:, :], ya5[:, :]),
            waits=[xld, ayn])
        prev_x = dx1
        stt = ln_stats(xs[:, tc * DM:(tc + 1) * DM], [dx1])
        ap5 = act(lambda s, tc=tc: s.activation(
            lntm5[:, :], xs[:, tc * DM:(tc + 1) * DM], AF.Identity,
            bias=st_sc[:, 5:6], scale=st_sc[:, 4:5]), waits=[stt])
        for b4 in range(4):
            ln2_tile_done[tc] = transpose4(
                lambda j, b4=b4: lntm5[:, (b4 * 4 + j) * 128:(b4 * 4 + j + 1) * 128],
                [(lnx_fm, (b4 * 4 + j) * TPC + tc * 128) for j in range(4)],
                [ap5, p3_done_pe, p4_pe])
    if "d_ln2x" in dump_t:
        sdma("dmp", lambda s: s.dma_start(out=dump_t["d_ln2x"][:, :], in_=lnx_fm[:, :]),
             waits=[ln2_tile_done[NCH - 1]])
    if "d_xs1" in dump_t:
        sdma("dmp", lambda s: s.dma_start(out=dump_t["d_xs1"][:, :], in_=xs[:, :]),
             waits=[prev_x])

    p5a_pe = ("pe", S.val("pe"))
    p5a_act = ("act", S.val("act"))
    p5a_dve = ("dve", S.val("dve"))
    S.checkpoint("p5a")

    p5a_stack.close()
    p5b_stack = contextlib.ExitStack()

    def sb5b(name, shape, dtype):
        return p5b_stack.enter_context(nc.sbuf_tensor(name, shape, dtype))

    fuse = sb5b("fuse", [128, NIT * 256], BF16)
    wg_b = [sb5b(f"wg_b{i}", [128, 4096], BF16) for i in range(2)]
    wu_b = [sb5b(f"wu_b{i}", [128, 4096], BF16) for i in range(2)]
    wd_b = [sb5b(f"wd_b{i}", [128, 1024], BF16) for i in range(2)]
    sg_st = [sb5b(f"sg_st{i}", [128, 256], BF16) for i in range(2)]
    uh_st = [sb5b(f"uh_st{i}", [128, 256], BF16) for i in range(2)]
    zst = [sb5b(f"zst{i}", [128, DM], F32) for i in range(2)]
    sdown_bc = sb5b("sdown_s", [128, DM], BF16)

    p5a_barrier = [p5a_pe, p5a_act, p5a_dve]
    sdown_ld = gdma("sd5", lambda g: g.dma_start(out=sdown_bc[:, :], in_=sdown_d[:, :]),
                    waits=p5a_barrier)
    wg_last = [p5a_barrier, p5a_barrier]
    wu_last = [p5a_barrier, p5a_barrier]
    wd_last = [p5a_barrier, p5a_barrier]
    g_ps, u_ps = ps_a, ps_b
    g_ps_last = A_last
    u_ps_last = y_ps_last[0]
    d_ps = {0: ps_c, 1: ps_d}
    d_ps_last = {0: O_last, 1: y_ps_last[1]}
    zst_free = [None, None]
    out_marks = []
    fuse_dump_mark = [None]

    for g in range(4):
        lnc_rdy = ln2_tile_done[2 * g + 1]
        lnc_rdy0 = ln2_tile_done[2 * g]
        fuse_done = None
        fuse_by_isub = [fuse_dump_mark[0], fuse_dump_mark[0]]
        gmm = umm = None
        for ib in range(32):
            bi = ib % 2
            wgl = gdma(f"wg{bi}", lambda gp, ib=ib, bi=bi: gp.dma_start(
                out=wg_b[bi].ap().rearrange("p (q m i) -> p q m i", q=4, m=4),
                in_=wg5[ib].rearrange("q f m i -> f q m i")), waits=wg_last[bi])
            wul = gdma(f"wu{bi}", lambda gp, ib=ib, bi=bi: gp.dma_start(
                out=wu_b[bi].ap().rearrange("p (q m i) -> p q m i", q=4, m=4),
                in_=wu5[ib].rearrange("q f m i -> f q m i")), waits=wu_last[bi])
            for isub in range(2):
                it = ib * 2 + isub
                waits = [wgl, lnc_rdy, lnc_rdy0]
                if g_ps_last is not None:
                    waits.append(g_ps_last)
                for kt in range(KT):
                    col = (kt // 4) * 1024 + (kt % 4) * 256 + isub * 128
                    gmm = pe(lambda t, col=col, kt=kt, bi=bi, g=g: t.matmul(
                        g_ps[:, 0:256], wg_b[bi][:, col:col + 128],
                        lnx_fm[:, kt * TPC + g * 256: kt * TPC + (g + 1) * 256],
                        start=(kt == 0), stop=(kt == KT - 1)),
                        waits=waits if kt == 0 else [], inc=(kt == KT - 1))
                waits = [wul, lnc_rdy, lnc_rdy0]
                if u_ps_last is not None:
                    waits.append(u_ps_last)
                for kt in range(KT):
                    col = (kt // 4) * 1024 + (kt % 4) * 256 + isub * 128
                    umm = pe(lambda t, col=col, kt=kt, bi=bi, g=g: t.matmul(
                        u_ps[:, 0:256], wu_b[bi][:, col:col + 128],
                        lnx_fm[:, kt * TPC + g * 256: kt * TPC + (g + 1) * 256],
                        start=(kt == 0), stop=(kt == KT - 1)),
                        waits=waits if kt == 0 else [], inc=(kt == KT - 1))
                war = [fuse_by_isub[isub]] if fuse_by_isub[isub] else []
                asg = act(lambda s, it=it, isub=isub: s.activation(
                    sg_st[isub][:, :], g_ps[:, 0:256], AF.Silu,
                    bias=0.0, scale=sg_t[:, it:it + 1]),
                    waits=[gmm, consts_s] + war)
                g_ps_last = asg
                auh = act(lambda s, it=it, isub=isub: s.activation(
                    uh_st[isub][:, :], u_ps[:, 0:256], AF.Identity,
                    bias=0.0, scale=su_t[:, it:it + 1]), waits=[umm] + war)
                u_ps_last = auh
                fuse_done = dve(lambda v, it=it, isub=isub: v.tensor_mul(
                    fuse[:, it * 256:(it + 1) * 256],
                    sg_st[isub][:, :], uh_st[isub][:, :]), waits=[asg, auh])
                fuse_by_isub[isub] = fuse_done
            wg_last[bi] = [gmm]
            wu_last[bi] = [umm]
        if g == 0 and "d_fuse" in dump_t:
            fdm = sdma("dmp", lambda s: s.dma_start(out=dump_t["d_fuse"][:, :],
                                                    in_=fuse[:, :]),
                       waits=[fuse_done])
            fuse_dump_mark[0] = fdm
        for oc in range(4):
            dmm = {0: None, 1: None}
            for itp in range(32):
                bi = itp % 2
                wdl = gdma(f"wd{bi}", lambda gp, oc=oc, itp=itp, bi=bi: gp.dma_start(
                    out=wd_b[bi].ap().rearrange("p (m o) -> p m o", m=2),
                    in_=wd5[oc, itp]), waits=wd_last[bi])
                for itm in range(2):
                    it = itp * 2 + itm
                    for tsub in range(2):
                        ps = d_ps[tsub]
                        waits = []
                        if itm == 0 and tsub == 0:
                            waits.append(wdl)
                        if it == 0:
                            waits.append(fuse_done)
                            if d_ps_last[tsub] is not None:
                                waits.append(d_ps_last[tsub])
                        dmm[tsub] = pe(
                            lambda t, ps=ps, it=it, itm=itm, tsub=tsub, bi=bi:
                            t.matmul(
                                ps[:, 0:512],
                                fuse[:, it * 256 + tsub * 128:
                                     it * 256 + tsub * 128 + 128],
                                wd_b[bi][:, itm * 512:(itm + 1) * 512],
                                start=(it == 0), stop=(it == NIT - 1)),
                            waits=waits,
                            inc=(itm == 1 and tsub == 1) or (it == NIT - 1))
                wd_last[bi] = [dmm[1]]
            for tsub in range(2):
                ps = d_ps[tsub]
                waits = [dmm[tsub], sdown_ld]
                if zst_free[tsub] is not None:
                    waits.append(zst_free[tsub])
                dz = dve(lambda v, ps=ps, oc=oc, tsub=tsub: v.tensor_mul(
                    zst[tsub][:, oc * 512:(oc + 1) * 512],
                    ps[:, 0:512], sdown_bc[:, oc * 512:(oc + 1) * 512]),
                    waits=waits)
                d_ps_last[tsub] = dz
                act(lambda s, oc=oc, tsub=tsub: s.activation(
                    junk_a[:, 0:512], zst[tsub][:, oc * 512:(oc + 1) * 512],
                    AF.Identity, bias=0.0, scale=1.0,
                    accum_out=sp1[:, tsub * 4 + oc:tsub * 4 + oc + 1]),
                    waits=[dz])
                dve(lambda v, oc=oc, tsub=tsub: v.scalar_tensor_tensor(
                    junk_s[:, 0:512], zst[tsub][:, oc * 512:(oc + 1) * 512], 1.0,
                    zst[tsub][:, oc * 512:(oc + 1) * 512],
                    ALU.mult, ALU.mult,
                    accum_out=sp2[:, tsub * 4 + oc:tsub * 4 + oc + 1]),
                    waits=[dz])
        zdump = []
        if g == 0 and "d_z" in dump_t:
            zw = [("act", S.val("act")), ("dve", S.val("dve"))]
            for ts_ in range(2):
                zdump.append(sdma("dmp2", lambda s, ts_=ts_: s.dma_start(
                    out=dump_t["d_z"][:, ts_ * DM:(ts_ + 1) * DM],
                    in_=zst[ts_][:, :]), waits=zw))
        stats_in = [("act", S.val("act")), ("dve", S.val("dve"))]
        for tsub in range(2):
            tc = g * 2 + tsub
            dve(lambda v, tsub=tsub: v.reduce_sum(
                st_sc[:, 0:1], sp1[:, tsub * 4:tsub * 4 + 4], axis=AX.X),
                waits=stats_in)
            dve(lambda v, tsub=tsub: v.reduce_sum(
                st_sc[:, 1:2], sp2[:, tsub * 4:tsub * 4 + 4], axis=AX.X))
            dve(lambda v: v.tensor_scalar_mul(st_sc[:, 2:3], st_sc[:, 0:1],
                                              1.0 / DM))
            dve(lambda v: v.tensor_mul(st_sc[:, 6:7], st_sc[:, 2:3],
                                       st_sc[:, 2:3]))
            d4v = dve(lambda v: v.scalar_tensor_tensor(
                st_sc[:, 3:4], st_sc[:, 1:2], 1.0 / DM, st_sc[:, 6:7],
                ALU.mult, ALU.subtract))
            d4v = dve(lambda v: v.tensor_scalar_add(st_sc[:, 3:4], st_sc[:, 3:4],
                                                    EPS_LN))
            act(lambda s: s.activation(st_sc[:, 7:8], st_sc[:, 3:4], AF.Ln,
                                       bias=0.0, scale=1.0), waits=[d4v])
            a3 = act(lambda s: s.activation(st_sc[:, 4:5], st_sc[:, 7:8],
                                            AF.Exp, bias=0.0, scale=-0.5))
            d5v = dve(lambda v: v.scalar_tensor_tensor(
                st_sc[:, 5:6], st_sc[:, 2:3], -1.0, st_sc[:, 4:5],
                ALU.mult, ALU.mult), waits=[a3])
            azn = act(lambda s, tsub=tsub: s.activation(
                zst[tsub][:, :], zst[tsub][:, :], AF.Identity,
                bias=st_sc[:, 5:6], scale=st_sc[:, 4:5]),
                waits=[d5v] + (zdump[-1:] if zdump else []))
            dxf = dve(lambda v, tc=tc, tsub=tsub: v.tensor_add(
                xs[:, tc * DM:(tc + 1) * DM], xs[:, tc * DM:(tc + 1) * DM],
                zst[tsub][:, :]), waits=[azn])
            stm = sdma("out", lambda s, tc=tc: s.dma_start(
                out=yout[tc * 128:(tc + 1) * 128, :],
                in_=xs[:, tc * DM:(tc + 1) * DM]), waits=[dxf])
            zst_free[tsub] = dxf
            out_marks.append(stm)

    if stop_after is not None:
        lens, cnts = S.marks[stop_after]
        for e in ENGINES:
            S.recs[e] = S.recs[e][:lens[e]]
        S.cnt = cnts
    for sem in SY_DMA_SEMS:
        v = S.val(sem)
        if v > 0:
            S.emit("sy", lambda s, sem=sem, v=v: s.wait_ge(sems[sem], v))
    for sem in GP_DMA_SEMS:
        v = S.val(sem)
        if v > 0:
            S.emit("gp", lambda g, sem=sem, v=v: g.wait_ge(sems[sem], v))

    serialized = ("dve", "act")

    def replay(ename, eng):
        last_wait = {}
        nprev = 0
        for fn, waits, inc_sem, amt in S.recs[ename]:
            if ename in serialized and nprev > 0:
                eng.wait_ge(sems[ename], nprev)
            for (sname, v) in waits:
                if sname == ename and sname in COMPUTE_SEMS:
                    continue
                if last_wait.get(sname, -1) >= v:
                    continue
                eng.wait_ge(sems[sname], v)
                last_wait[sname] = v
            inst = fn(eng)
            if inc_sem is not None:
                if amt is None:
                    inst.then_inc(sems[inc_sem])
                else:
                    inst.then_inc(sems[inc_sem], amt)
            if inc_sem == ename:
                nprev += 1

    with nc.Block() as block:
        @block.tensor
        def _(eng):
            replay("pe", eng)

        @block.scalar
        def _(eng):
            replay("act", eng)

        @block.vector
        def _(eng):
            replay("dve", eng)

        @block.gpsimd
        def _(eng):
            replay("gp", eng)

        @block.sync
        def _(eng):
            replay("sy", eng)

    p5b_stack.close()
    stack.close()
    return nc


_NC = None
_NC_KEY = None


def _get_program(dumps=()):
    global _NC, _NC_KEY
    key = tuple(sorted(dumps))
    if _NC is None or _NC_KEY != key:
        _NC = build_program(dumps=key)
        _NC_KEY = key
    return _NC


def _prep_shared(inputs):
    d = {}
    for p, name in [("q", "L0_attn_Wq"), ("k", "L0_attn_Wk"),
                    ("v", "L0_attn_Wv"), ("o", "L0_attn_Wo")]:
        W = np.asarray(inputs[name])
        d[f"w4{p}"] = np.ascontiguousarray(
            W.reshape(KT, 128, 8, 256).transpose(2, 0, 1, 3))
    d["wg5"] = np.ascontiguousarray(
        np.asarray(inputs["L0_ffn_w_gate"]).reshape(4, 4, 128, 32, 256)
        .transpose(3, 0, 2, 1, 4))
    d["wu5"] = np.ascontiguousarray(
        np.asarray(inputs["L0_ffn_w_up"]).reshape(4, 4, 128, 32, 256)
        .transpose(3, 0, 2, 1, 4))
    d["wd5"] = np.ascontiguousarray(
        np.asarray(inputs["L0_ffn_w_down"]).reshape(32, 2, 128, 4, 512)
        .transpose(3, 0, 2, 1, 4))
    d["bqkv"] = np.ascontiguousarray(np.stack(
        [np.asarray(inputs["L0_attn_bq"]).reshape(KT, 128),
         np.asarray(inputs["L0_attn_bk"]).reshape(KT, 128),
         np.asarray(inputs["L0_attn_bv"]).reshape(KT, 128)],
        axis=0).reshape(48, 128).T)
    d["bo_bc"] = np.ascontiguousarray(
        np.broadcast_to(np.asarray(inputs["L0_attn_bo"]), (128, DM)))
    d["sdown_bc"] = np.ascontiguousarray(
        np.broadcast_to(np.asarray(inputs["L0_ffn_s_down"]), (128, DM)))
    d["sg_t"] = np.ascontiguousarray(
        np.asarray(inputs["L0_ffn_s_gate"]).reshape(NIT, 128).T)
    d["su_t"] = np.ascontiguousarray(
        np.asarray(inputs["L0_ffn_s_up"]).reshape(NIT, 128).T)
    d["id128"] = np.eye(128, dtype=np.float32)
    d["triu"] = np.triu(np.ones((128, 128), dtype=np.float32))
    return d


def make_in_maps(inputs):
    shared = _prep_shared(inputs)
    x = np.asarray(inputs["L0_attn_x"])
    in_maps = []
    for c in range(8):
        b, qi = c // 4, c % 4
        m = dict(shared)
        m["xc"] = np.ascontiguousarray(x[b, qi * TPC:(qi + 1) * TPC, :])
        pm = np.zeros((128, 4), dtype=np.float32)
        pm[:, :qi] = 1.0
        m["pmask"] = pm
        in_maps.append(m)
    return in_maps


def kernel(**inputs):
    nc = _get_program()
    in_maps = make_in_maps(inputs)
    res = run_bass_kernel_spmd(nc, in_maps, core_ids=list(range(8)))
    out = np.empty((2, 4096, DM), dtype=np.float32)
    for c in range(8):
        b, qi = c // 4, c % 4
        out[b, qi * TPC:(qi + 1) * TPC, :] = res.results[c]["yout"]
    return out


if __name__ == "__main__":
    import sys
    nc = build_program(dumps=tuple(sys.argv[1:]))
    print("build ok")


# version 04
# speedup vs baseline: 1.0048x; 1.0048x over previous
import contextlib
import numpy as np
import concourse.bass as bass
import concourse.mybir as mybir
from concourse.bass_utils import run_bass_kernel_spmd

F32 = mybir.dt.float32
BF16 = mybir.dt.bfloat16
I8 = mybir.dt.int8
AF = mybir.ActivationFunctionType
ALU = mybir.AluOpType
AX = mybir.AxisListType

TPC = 1024
NCH = 8
H = 16
DM = 2048
KT = 16
NIT = 64
EPS_LN = 1e-5
EPS_Z = 1e-6
VST = 130

ENGINES = ("pe", "act", "dve", "gp", "sy")
COMPUTE_SEMS = {"pe", "act", "dve"}


class Sched:

    def __init__(self):
        self.recs = {e: [] for e in ENGINES}
        self.cnt = {}
        self.marks = {}

    def checkpoint(self, name):
        self.marks[name] = ({e: len(self.recs[e]) for e in ENGINES},
                            dict(self.cnt))

    def emit(self, eng, fn, waits=(), inc=None, amt=1):
        waits = [w for w in waits if w is not None]
        waits = [(s, v) for (s, v) in waits if v > 0]
        mark = None
        if inc is not None:
            self.cnt[inc] = self.cnt.get(inc, 0) + (amt if amt else 1)
            mark = (inc, self.cnt[inc])
        self.recs[eng].append((fn, waits, inc if mark else None, amt))
        return mark

    def val(self, sem):
        return self.cnt.get(sem, 0)


def build_program(dumps=(), stop_after=None):
    nc = bass.Bass("TRN2", target_bir_lowering=False)

    xc = nc.dram_tensor("xc", [TPC, DM], F32, kind="ExternalInput").ap()
    w4 = {p: nc.dram_tensor(f"w4{p}", [8, KT, 128, 256], F32, kind="ExternalInput").ap()
          for p in "qkvo"}
    wg5 = nc.dram_tensor("wg5", [32, 4, 128, 4, 256], I8, kind="ExternalInput").ap()
    wu5 = nc.dram_tensor("wu5", [32, 4, 128, 4, 256], I8, kind="ExternalInput").ap()
    wd5 = nc.dram_tensor("wd5", [4, 32, 128, 2, 512], I8, kind="ExternalInput").ap()
    bqkv_d = nc.dram_tensor("bqkv", [128, 48], F32, kind="ExternalInput").ap()
    bo_bc_d = nc.dram_tensor("bo_bc", [128, DM], F32, kind="ExternalInput").ap()
    sdown_d = nc.dram_tensor("sdown_bc", [128, DM], F32, kind="ExternalInput").ap()
    sg_d = nc.dram_tensor("sg_t", [128, NIT], F32, kind="ExternalInput").ap()
    su_d = nc.dram_tensor("su_t", [128, NIT], F32, kind="ExternalInput").ap()
    pm_d = nc.dram_tensor("pmask", [128, 4], F32, kind="ExternalInput").ap()
    id_d = nc.dram_tensor("id128", [128, 128], F32, kind="ExternalInput").ap()
    tri_d = nc.dram_tensor("triu", [128, 128], F32, kind="ExternalInput").ap()

    yout = nc.dram_tensor("yout", [TPC, DM], F32, kind="ExternalOutput").ap()
    dump_specs = {
        "d_lnx": ([128, KT * TPC], BF16), "d_kftm": ([128, NCH * H * 128], BF16),
        "d_vtm": ([128, NCH * H * VST], BF16), "d_kvtot": ([128, H * VST], F32),
        "d_state0": ([128, H * VST], F32), "d_qf": ([128, H * TPC], BF16),
        "d_yscr": ([TPC, DM], F32), "d_ln2x": ([128, KT * TPC], BF16),
        "d_xs1": ([128, NCH * DM], F32),
        "d_xa": ([128, DM], F32), "d_stsc": ([128, 8], F32),
        "d_fuse": ([128, NIT * 256], BF16), "d_z": ([128, 2 * DM], F32),
        "d_lntm": ([128, DM], BF16), "d_sp": ([128, 32], F32),
    }
    dump_t = {n: nc.dram_tensor(n, s, d, kind="ExternalOutput").ap()
              for n, (s, d) in dump_specs.items() if n in dumps}

    y_scr = nc.dram_tensor("y_scr", [TPC, DM], F32, kind="Internal").ap()
    cc_in = nc.dram_tensor("cc_in", [128, H * VST], F32, kind="Internal")
    cc_out = nc.dram_tensor("cc_out", [4 * 128, H * VST], F32, kind="Internal")

    S = Sched()
    stack = contextlib.ExitStack()

    def sb(name, shape, dtype):
        return stack.enter_context(nc.sbuf_tensor(name, shape, dtype))

    lnx_fm = sb("lnx_fm", [128, KT * TPC], BF16)
    ident = sb("ident", [128, 128], BF16)
    triu = sb("triu_s", [128, 128], BF16)
    bqkv = sb("bqkv_s", [128, 48], F32)
    bo_bc = sb("bo_bc_s", [128, DM], BF16)
    sg_t = sb("sg_s", [128, NIT], F32)
    su_t = sb("su_s", [128, NIT], F32)
    pmask = sb("pmask_s", [128, 4], F32)
    kv_state = sb("kv_state", [128, H * VST], F32)
    state_bf = sb("state_bf", [128, H * VST], BF16)
    mb_arr = sb("mb_arr", [128, NCH], F32)
    rb_arr = sb("rb_arr", [128, NCH], F32)
    nb_arr = sb("nb_arr", [128, NCH], F32)
    sp1 = sb("sp1", [128, 32], F32)
    sp2 = sb("sp2", [128, 32], F32)
    st_sc = sb("st_sc", [128, 8], F32)
    rz_sc = sb("rz_sc", [128, 2], F32)
    junk_s = sb("junk_s", [128, 512], BF16)
    junk_a = sb("junk_a", [128, 512], BF16)

    GP_DMA_SEMS = ["cg", "w0", "w1", "wg0", "wg1", "wu0", "wu1",
                   "wd0", "wd1", "sd5", "gdmp"]
    SY_DMA_SEMS = ["cs", "xa", "dmp", "dmp2", "ccin", "gath", "ysl0", "ysl1",
                   "x5", "y5", "out"]
    sem_names = ["pe", "act", "dve", "cc"] + GP_DMA_SEMS + SY_DMA_SEMS
    sems = {n: stack.enter_context(nc.semaphore(f"s_{n}")) for n in sem_names}

    def gdma(sem, fn, waits=()):
        return S.emit("gp", fn, waits=waits, inc=sem, amt=16)

    def sdma(sem, fn, waits=()):
        return S.emit("sy", fn, waits=waits, inc=sem, amt=16)

    def pe(fn, waits=(), inc=True):
        return S.emit("pe", fn, waits=waits, inc="pe" if inc else None)

    def act(fn, waits=(), inc=True):
        return S.emit("act", fn, waits=waits, inc="act")

    def dve(fn, waits=(), inc=True):
        return S.emit("dve", fn, waits=waits, inc="dve")

    att_stack = contextlib.ExitStack()

    def sba(name, shape, dtype):
        return att_stack.enter_context(nc.sbuf_tensor(name, shape, dtype))

    qf_fm = sba("qf_fm", [128, H * TPC], BF16)
    kf_tm = sba("kf_tm", [128, NCH * H * 128], BF16)
    v_tm = sba("v_tm", [128, NCH * H * VST], BF16)
    out_fm = sba("out_fm", [128, H * 4 * 128], BF16)
    kv_total = sba("kv_total", [128, H * VST], F32)
    wbuf = [sba(f"wbuf{i}", [128, KT * 256], BF16) for i in range(2)]
    r_st = [sba(f"r_st{i}", [128, 512], BF16) for i in range(2)]
    fm_st = [sba(f"fm_st{i}", [128, 512], BF16) for i in range(2)]
    kfT_st = [sba(f"kfT{i}", [128, 128], BF16) for i in range(2)]
    AT_st = [sba(f"AT{i}", [128, 128], BF16) for i in range(2)]
    xa_p1 = sba("xa_p1", [128, DM], F32)
    lntm_st = sba("lntm_st", [128, DM], BF16)
    ysl_st = [sba(f"ysl{i}", [128, 256], F32) for i in range(2)]

    ps_a = stack.enter_context(nc.psum_tensor("ps_a", [128, 512], F32))
    ps_b = stack.enter_context(nc.psum_tensor("ps_b", [128, 512], F32))
    ps_c = stack.enter_context(nc.psum_tensor("ps_c", [128, 512], F32))
    ps_d = stack.enter_context(nc.psum_tensor("ps_d", [128, 512], F32))
    ps_t1 = stack.enter_context(nc.psum_tensor("ps_t1", [128, 512], BF16))
    ps_t2 = stack.enter_context(nc.psum_tensor("ps_t2", [128, 512], BF16))
    ps_s1 = stack.enter_context(nc.psum_tensor("ps_s1", [128, VST], F32))
    ps_s2 = stack.enter_context(nc.psum_tensor("ps_s2", [128, VST], F32))

    ps_war = {}

    def war(ps):
        m = ps_war.get(ps.name)
        return [m] if m else []

    def set_war(ps, mark):
        ps_war[ps.name] = mark

    gdma("cg", lambda g: g.dma_start(out=ident[:, :], in_=id_d[:, :]))
    gdma("cg", lambda g: g.dma_start(out=triu[:, :], in_=tri_d[:, :]))
    gdma("cg", lambda g: g.dma_start(out=bo_bc[:, :], in_=bo_bc_d[:, :]))
    consts_g = ("cg", S.val("cg"))
    sdma("cs", lambda s: s.dma_start(out=bqkv[:, :], in_=bqkv_d[:, :]))
    sdma("cs", lambda s: s.dma_start(out=sg_t[:, :], in_=sg_d[:, :]))
    sdma("cs", lambda s: s.dma_start(out=su_t[:, :], in_=su_d[:, :]))
    sdma("cs", lambda s: s.dma_start(out=pmask[:, :], in_=pm_d[:, :]))
    consts_s = ("cs", S.val("cs"))
    ones_m = dve(lambda v: v.memset(
        v_tm.ap().rearrange("p (nh s) -> p nh s", s=VST)[:, :, 128:VST], 1.0))
    kvt_z = dve(lambda v: v.memset(kv_total[:, :], 0.0))


    def ln_stats(src_full, waits_src):
        a1 = d1 = None
        for q in range(4):
            sl = src_full[:, q * 512:(q + 1) * 512]
            a1 = act(lambda s, sl=sl, q=q: s.activation(
                junk_a[:, 0:512], sl, AF.Identity, bias=0.0, scale=1.0,
                accum_out=sp1[:, 24 + q:25 + q]),
                waits=waits_src if q == 0 else [])
            d1 = dve(lambda v, sl=sl, q=q: v.scalar_tensor_tensor(
                junk_s[:, 0:512], sl, 1.0, sl, ALU.mult, ALU.mult,
                accum_out=sp2[:, 24 + q:25 + q]),
                waits=waits_src if q == 0 else [])
        dve(lambda v: v.reduce_sum(st_sc[:, 0:1], sp1[:, 24:28], axis=AX.X),
            waits=[a1, d1])
        dve(lambda v: v.reduce_sum(st_sc[:, 1:2], sp2[:, 24:28], axis=AX.X))
        dve(lambda v: v.tensor_scalar_mul(st_sc[:, 2:3], st_sc[:, 0:1], 1.0 / DM))
        dve(lambda v: v.tensor_mul(st_sc[:, 6:7], st_sc[:, 2:3], st_sc[:, 2:3]))
        d4 = dve(lambda v: v.scalar_tensor_tensor(
            st_sc[:, 3:4], st_sc[:, 1:2], 1.0 / DM, st_sc[:, 6:7],
            ALU.mult, ALU.subtract))
        d4 = dve(lambda v: v.tensor_scalar_add(st_sc[:, 3:4], st_sc[:, 3:4],
                                               EPS_LN))
        act(lambda s: s.activation(st_sc[:, 7:8], st_sc[:, 3:4], AF.Ln,
                                   bias=0.0, scale=1.0), waits=[d4])
        a3 = act(lambda s: s.activation(st_sc[:, 4:5], st_sc[:, 7:8], AF.Exp,
                                        bias=0.0, scale=-0.5))
        d5 = dve(lambda v: v.scalar_tensor_tensor(
            st_sc[:, 5:6], st_sc[:, 2:3], -1.0, st_sc[:, 4:5],
            ALU.mult, ALU.mult), waits=[a3])
        return d5

    tpose_ps = [ps_t1, ps_t2]
    tpose_last = [None, None]
    tp_rot = [0]

    def transpose4(src_ap_fn, dst_list, waits_src):
        pi = tp_rot[0] % 2
        tp_rot[0] += 1
        psT = tpose_ps[pi]
        waits = list(waits_src) + [consts_g]
        if tpose_last[pi] is not None:
            waits.append(tpose_last[pi])
        tr = None
        for j in range(4):
            tr = pe(lambda t, j=j, psT=psT: t.transpose(
                psT[:, j * 128:(j + 1) * 128], src_ap_fn(j), ident[:, :]),
                waits=waits if j == 0 else [])
        ev = None
        for j in range(4):
            buf, col = dst_list[j]
            ev = act(lambda s, j=j, psT=psT, buf=buf, col=col: s.activation(
                buf[:, col:col + 128], psT[:, j * 128:(j + 1) * 128],
                AF.Identity, bias=0.0, scale=1.0), waits=[tr] if j == 0 else [])
        tpose_last[pi] = ev
        return ev

    p1_last = None
    prev_consumed = None
    for tc in range(NCH):
        ld = sdma("xa", lambda s, tc=tc: s.dma_start(
            out=xa_p1[:, :], in_=xc[tc * 128:(tc + 1) * 128, :]),
            waits=[prev_consumed] if prev_consumed else [])
        stt = ln_stats(xa_p1, [ld])
        if tc == 0 and "d_xa" in dump_t:
            sdma("dmp", lambda s: s.dma_start(out=dump_t["d_xa"][:, :], in_=xa_p1[:, :]),
                 waits=[stt])
        if tc == 0 and "d_stsc" in dump_t:
            sdma("dmp", lambda s: s.dma_start(out=dump_t["d_stsc"][:, :], in_=st_sc[:, :]),
                 waits=[stt])
        if tc == 0 and "d_sp" in dump_t:
            sdma("dmp", lambda s: s.dma_start(out=dump_t["d_sp"][:, :], in_=sp1[:, :]),
                 waits=[stt])
        ap = act(lambda s: s.activation(
            lntm_st[:, :], xa_p1[:, :], AF.Identity,
            bias=st_sc[:, 5:6], scale=st_sc[:, 4:5]), waits=[stt])
        if tc == 0 and "d_lntm" in dump_t:
            sdma("dmp", lambda s: s.dma_start(out=dump_t["d_lntm"][:, :], in_=lntm_st[:, :]),
                 waits=[ap])
        prev_consumed = ap
        for b4 in range(4):
            p1_last = transpose4(
                lambda j, b4=b4: lntm_st[:, (b4 * 4 + j) * 128:(b4 * 4 + j + 1) * 128],
                [(lnx_fm, (b4 * 4 + j) * TPC + tc * 128) for j in range(4)],
                [ap])
    lnx_ready = p1_last

    if "d_lnx" in dump_t:
        sdma("dmp", lambda s: s.dma_start(out=dump_t["d_lnx"][:, :], in_=lnx_fm[:, :]),
             waits=[lnx_ready])
    S.checkpoint("p1")

    wlast_use = [None, None]
    proj_ps = [ps_a, ps_b]
    proj_ps_last = [None, None]

    def stream_attn_w(p, ob, bi):
        waits = []
        if wlast_use[bi] is not None:
            waits.append(wlast_use[bi])
        return gdma(f"w{bi}", lambda g, p=p, ob=ob, bi=bi: g.dma_start(
            out=wbuf[bi].ap().rearrange("p (kt o) -> p kt o", kt=KT),
            in_=w4[p][ob].rearrange("kt f o -> f kt o")), waits=waits)

    elu_n = [0]

    def elu_evac(ps, mm, bcol, st_i):
        first = elu_n[0] == 0
        elu_n[0] += 1
        a1 = act(lambda s: s.activation(
            r_st[st_i][:, :], ps[:, 0:512], AF.Relu,
            bias=bqkv[:, bcol:bcol + 1], scale=1.0), waits=[mm, consts_s])
        if first:
            S.checkpoint("e1")
        d1 = dve(lambda v: v.tensor_scalar(
            fm_st[st_i][:, :], ps[:, 0:512], bqkv[:, bcol:bcol + 1], 0.0,
            ALU.add, ALU.min), waits=[mm, consts_s, a1])
        if first:
            S.checkpoint("e2")
        a2 = act(lambda s: s.activation(
            fm_st[st_i][:, :], fm_st[st_i][:, :], AF.Exp, bias=0.0, scale=1.0),
            waits=[d1])
        if first:
            S.checkpoint("e3")
        d2 = dve(lambda v: v.tensor_add(
            fm_st[st_i][:, :], r_st[st_i][:, :], fm_st[st_i][:, :]),
            waits=[a1, a2])
        if first:
            S.checkpoint("e4")
        return d2

    def proj_pass(p, consume):
        for ob in range(8):
            if p == "k" and ob == 1:
                S.checkpoint("p2k0")
            bi = ob % 2
            wld = stream_attn_w(p, ob, bi)
            for th in range(2):
                for osub in range(2):
                    h = ob * 2 + osub
                    k = (th * 2 + osub) % 2
                    ps = proj_ps[k]
                    waits = [wld, lnx_ready]
                    if proj_ps_last[k] is not None:
                        waits.append(proj_ps_last[k])
                    mm = None
                    for kt in range(KT):
                        mm = pe(lambda t, ps=ps, kt=kt, bi=bi, osub=osub, th=th:
                                t.matmul(
                                    ps[:, 0:512],
                                    wbuf[bi][:, kt * 256 + osub * 128:
                                             kt * 256 + (osub + 1) * 128],
                                    lnx_fm[:, kt * TPC + th * 512:
                                           kt * TPC + (th + 1) * 512],
                                    start=(kt == 0), stop=(kt == KT - 1)),
                                waits=waits if kt == 0 else [],
                                inc=(kt == KT - 1))
                    wlast_use[bi] = mm
                    if p == "k" and ob == 0 and th == 0 and osub == 0:
                        S.checkpoint("p2m")
                    proj_ps_last[k] = consume(h, th, ps, mm, k)
                    if p == "k" and ob == 0 and th == 0 and osub == 0:
                        S.checkpoint("p2u")

    def consume_k(h, th, ps, mm, st_i):
        rdy = elu_evac(ps, mm, 16 + h, st_i)
        transpose4(
            lambda j: fm_st[st_i][:, j * 128:(j + 1) * 128],
            [(kf_tm, ((th * 4 + j) * H + h) * 128) for j in range(4)],
            [rdy])
        return rdy

    def consume_v(h, th, ps, mm, st_i):
        rdy = act(lambda s: s.activation(
            fm_st[st_i][:, :], ps[:, 0:512], AF.Identity,
            bias=bqkv[:, 32 + h:32 + h + 1], scale=1.0), waits=[mm, consts_s])
        transpose4(
            lambda j: fm_st[st_i][:, j * 128:(j + 1) * 128],
            [(v_tm, ((th * 4 + j) * H + h) * VST) for j in range(4)],
            [rdy])
        return rdy

    proj_pass("k", consume_k)
    S.checkpoint("p2k")
    proj_pass("v", consume_v)
    p2_done_act = ("act", S.val("act"))

    if "d_kftm" in dump_t:
        sdma("dmp", lambda s: s.dma_start(out=dump_t["d_kftm"][:, :], in_=kf_tm[:, :]),
             waits=[p2_done_act])
    if "d_vtm" in dump_t:
        sdma("dmp", lambda s: s.dma_start(out=dump_t["d_vtm"][:, :], in_=v_tm[:, :]),
             waits=[p2_done_act, ones_m])
    S.checkpoint("p2")

    tot_ps = [ps_s1, ps_s2]
    tot_last = [None, None]
    tot_ev = None
    for h in range(H):
        k = h % 2
        kps = tot_ps[k]
        waits = [p2_done_act, ones_m]
        if tot_last[k] is not None:
            waits.append(tot_last[k])
        mm = None
        for n in range(NCH):
            mm = pe(lambda t, kps=kps, n=n, h=h: t.matmul(
                kps[:, 0:129],
                kf_tm[:, (n * H + h) * 128:(n * H + h) * 128 + 128],
                v_tm[:, (n * H + h) * VST:(n * H + h) * VST + 129],
                start=(n == 0), stop=(n == NCH - 1)),
                waits=waits if n == 0 else [], inc=(n == NCH - 1))
        tot_ev = dve(lambda v, kps=kps, h=h: v.tensor_copy(
            kv_total[:, h * VST:h * VST + 129], kps[:, 0:129]), waits=[mm])
        tot_last[k] = tot_ev
    totals_done = tot_ev
    if "d_kvtot" in dump_t:
        sdma("dmp", lambda s: s.dma_start(out=dump_t["d_kvtot"][:, :], in_=kv_total[:, :]),
             waits=[totals_done])

    st_in = sdma("ccin", lambda s: s.dma_start(out=cc_in.ap()[:, :], in_=kv_total[:, :]),
                 waits=[totals_done])
    S.emit("gp", lambda g: g.collective_compute(
        "AllGather", ALU.bypass,
        replica_groups=[[0, 1, 2, 3], [4, 5, 6, 7]],
        ins=[cc_in.ap().opt()], outs=[cc_out.ap().opt()]),
        waits=[st_in], inc="cc", amt=None)
    cc_done = ("cc", 1)
    S.checkpoint("p2b")

    qf_last = [None]

    def consume_q(h, th, ps, mm, st_i):
        rdy = elu_evac(ps, mm, h, st_i)
        qf_last[0] = dve(lambda v, h=h, th=th, st_i=st_i: v.tensor_copy(
            qf_fm[:, h * TPC + th * 512: h * TPC + (th + 1) * 512],
            fm_st[st_i][:, :]), waits=[rdy])
        return qf_last[0]

    proj_pass("q", consume_q)
    qf_done = qf_last[0]
    p3_done_pe = ("pe", S.val("pe"))

    if "d_qf" in dump_t:
        sdma("dmp", lambda s: s.dma_start(out=dump_t["d_qf"][:, :], in_=qf_fm[:, :]),
             waits=[qf_done])
    S.checkpoint("p3")

    pf = dve(lambda v: v.memset(kv_state[:, :], 0.0), waits=[cc_done, totals_done])
    for r in range(4):
        gl = sdma("gath", lambda s, r=r: s.dma_start(
            out=kv_total[:, :], in_=cc_out.ap()[r * 128:(r + 1) * 128, :]),
            waits=[cc_done] + ([pf] if r > 0 else []))
        pf = dve(lambda v, r=r: v.scalar_tensor_tensor(
            kv_state[:, :], kv_total[:, :], pmask[:, r:r + 1], kv_state[:, :],
            ALU.mult, ALU.add), waits=[gl, consts_s])
    st_f32 = pf
    st_upd = act(lambda s: s.activation(state_bf[:, :], kv_state[:, :],
                                        AF.Identity, bias=0.0, scale=1.0),
                 waits=[pf])
    if "d_state0" in dump_t:
        sdma("dmp", lambda s: s.dma_start(out=dump_t["d_state0"][:, :], in_=kv_state[:, :]),
             waits=[pf])
    S.checkpoint("p3b")

    kfT_last = tpose_last[0]
    A_last = proj_ps_last[0]
    O_last = None
    oT_last = tpose_last[1]
    kv_ps_last = [tot_last[0], tot_last[1]]
    y_ps_last = [proj_ps_last[1], None]
    yps = [ps_b, ps_d]
    ysl_dma = [None, None]

    for grp in range(2):
        for n4 in range(4):
            n = grp * 4 + n4
            for h in range(H):
                si = (n * H + h) % 2
                waits = [p2_done_act]
                if kfT_last is not None:
                    waits.append(kfT_last)
                t1 = pe(lambda t, n=n, h=h: t.transpose(
                    ps_t1[:, 0:128],
                    kf_tm[:, (n * H + h) * 128:(n * H + h) * 128 + 128],
                    ident[:, :]), waits=waits + [consts_g])
                e1 = act(lambda s, si=si: s.activation(
                    kfT_st[si][:, :], ps_t1[:, 0:128], AF.Identity,
                    bias=0.0, scale=1.0), waits=[t1])
                kfT_last = e1
                waits = [e1, qf_done]
                if A_last is not None:
                    waits.append(A_last)
                m1 = pe(lambda t, n=n, h=h, si=si: t.matmul(
                    ps_a[:, 0:128], kfT_st[si][:, :],
                    qf_fm[:, h * TPC + n * 128: h * TPC + (n + 1) * 128],
                    start=True, stop=True), waits=waits)
                d1 = dve(lambda v, si=si: v.tensor_mul(
                    AT_st[si][:, :], ps_a[:, 0:128], triu[:, :]),
                    waits=[m1, consts_g])
                A_last = d1
                waits = [d1, st_upd]
                if O_last is not None:
                    waits.append(O_last)
                m2 = pe(lambda t, n=n, h=h, si=si: t.matmul(
                    ps_c[:, 0:129], AT_st[si][:, :],
                    v_tm[:, (n * H + h) * VST:(n * H + h) * VST + 129],
                    start=True, stop=False), waits=waits)
                m3 = pe(lambda t, n=n, h=h: t.matmul(
                    ps_c[:, 0:129],
                    qf_fm[:, h * TPC + n * 128: h * TPC + (n + 1) * 128],
                    state_bf[:, h * VST:h * VST + 129],
                    start=False, stop=True), inc=True)
                dve(lambda v: v.tensor_scalar_add(
                    rz_sc[:, 0:1], ps_c[:, 128:129], EPS_Z), waits=[m3])
                d3 = dve(lambda v: v.reciprocal(rz_sc[:, 1:2], rz_sc[:, 0:1]))
                d4 = dve(lambda v, si=si: v.tensor_scalar_mul(
                    AT_st[si][:, :], ps_c[:, 0:128], rz_sc[:, 1:2]), waits=[d3])
                O_last = d4
                waits = [d4]
                if oT_last is not None:
                    waits.append(oT_last)
                t2 = pe(lambda t, si=si: t.transpose(
                    ps_t2[:, 0:128], AT_st[si][:, :], ident[:, :]), waits=waits)
                e2 = act(lambda s, n4=n4, h=h: s.activation(
                    out_fm[:, (h * 4 + n4) * 128:(h * 4 + n4) * 128 + 128],
                    ps_t2[:, 0:128], AF.Identity, bias=0.0, scale=1.0),
                    waits=[t2])
                oT_last = e2
                ki = h % 2
                kps = tot_ps[ki]
                waits = []
                if kv_ps_last[ki] is not None:
                    waits.append(kv_ps_last[ki])
                m4 = pe(lambda t, n=n, h=h, kps=kps: t.matmul(
                    kps[:, 0:129],
                    kf_tm[:, (n * H + h) * 128:(n * H + h) * 128 + 128],
                    v_tm[:, (n * H + h) * VST:(n * H + h) * VST + 129],
                    start=True, stop=True), waits=waits)
                d5 = dve(lambda v, h=h, kps=kps: v.tensor_add(
                    kv_state[:, h * VST:h * VST + 129],
                    kv_state[:, h * VST:h * VST + 129], kps[:, 0:129]),
                    waits=[m4, st_upd])
                kv_ps_last[ki] = d5
                st_upd = act(lambda s, h=h: s.activation(
                    state_bf[:, h * VST:h * VST + 129],
                    kv_state[:, h * VST:h * VST + 129],
                    AF.Identity, bias=0.0, scale=1.0), waits=[d5])
        grp_out_ready = oT_last
        for ob in range(8):
            bi = ob % 2
            waits = [grp_out_ready]
            if wlast_use[bi] is not None:
                waits.append(wlast_use[bi])
            wld = gdma(f"w{bi}", lambda g, ob=ob, bi=bi: g.dma_start(
                out=wbuf[bi].ap().rearrange("p (kt o) -> p kt o", kt=KT),
                in_=w4["o"][ob].rearrange("kt f o -> f kt o")), waits=waits)
            for n4 in range(4):
                n = grp * 4 + n4
                yk = n4 % 2
                ps = yps[yk]
                waits = [wld, grp_out_ready, consts_g]
                if y_ps_last[yk] is not None:
                    waits.append(y_ps_last[yk])
                mm = None
                for kt in range(KT):
                    mm = pe(lambda t, ps=ps, kt=kt, n4=n4, bi=bi: t.matmul(
                        ps[:, 0:256],
                        out_fm[:, (kt * 4 + n4) * 128:(kt * 4 + n4) * 128 + 128],
                        wbuf[bi][:, kt * 256:(kt + 1) * 256],
                        start=(kt == 0), stop=(kt == KT - 1)),
                        waits=waits if kt == 0 else [], inc=(kt == KT - 1))
                wlast_use[bi] = mm
                yi = (ob * 4 + n4) % 2
                waits = [mm]
                if ysl_dma[yi] is not None:
                    waits.append(ysl_dma[yi])
                d6 = dve(lambda v, ps=ps, ob=ob, yi=yi: v.tensor_add(
                    ysl_st[yi][:, :], ps[:, 0:256],
                    bo_bc[:, ob * 256:(ob + 1) * 256]), waits=waits)
                y_ps_last[yk] = d6
                a7 = act(lambda s, ob=ob, n4=n4, yi=yi: s.activation(
                    junk_a[:, 0:256], ysl_st[yi][:, :],
                    AF.Identity, bias=0.0, scale=1.0,
                    accum_out=sp1[:, n4 * 8 + ob:n4 * 8 + ob + 1]), waits=[d6])
                d7 = dve(lambda v, ob=ob, n4=n4, yi=yi: v.scalar_tensor_tensor(
                    junk_s[:, 0:256], ysl_st[yi][:, :], 1.0, ysl_st[yi][:, :],
                    ALU.mult, ALU.mult,
                    accum_out=sp2[:, n4 * 8 + ob:n4 * 8 + ob + 1]), waits=[d6])
                ysl_dma[yi] = sdma(f"ysl{yi}", lambda s, n=n, ob=ob, yi=yi: s.dma_start(
                    out=y_scr[n * 128:(n + 1) * 128, ob * 256:(ob + 1) * 256],
                    in_=ysl_st[yi][:, :]), waits=[d7, a7])
        stats_in = [("act", S.val("act")), ("dve", S.val("dve"))]
        for n4 in range(4):
            n = grp * 4 + n4
            dve(lambda v, n4=n4: v.reduce_sum(
                st_sc[:, 0:1], sp1[:, n4 * 8:n4 * 8 + 8], axis=AX.X),
                waits=stats_in)
            dve(lambda v, n4=n4: v.reduce_sum(
                st_sc[:, 1:2], sp2[:, n4 * 8:n4 * 8 + 8], axis=AX.X))
            dve(lambda v, n=n: v.tensor_scalar_mul(
                mb_arr[:, n:n + 1], st_sc[:, 0:1], 1.0 / DM))
            dve(lambda v, n=n: v.tensor_mul(
                st_sc[:, 6:7], mb_arr[:, n:n + 1], mb_arr[:, n:n + 1]))
            d4v = dve(lambda v: v.scalar_tensor_tensor(
                st_sc[:, 3:4], st_sc[:, 1:2], 1.0 / DM, st_sc[:, 6:7],
                ALU.mult, ALU.subtract))
            d4v = dve(lambda v: v.tensor_scalar_add(st_sc[:, 3:4], st_sc[:, 3:4],
                                                    EPS_LN))
            act(lambda s: s.activation(st_sc[:, 7:8], st_sc[:, 3:4], AF.Ln,
                                       bias=0.0, scale=1.0), waits=[d4v])
            a3 = act(lambda s, n=n: s.activation(
                rb_arr[:, n:n + 1], st_sc[:, 7:8], AF.Exp, bias=0.0, scale=-0.5))
            dve(lambda v, n=n: v.scalar_tensor_tensor(
                nb_arr[:, n:n + 1], mb_arr[:, n:n + 1], -1.0, rb_arr[:, n:n + 1],
                ALU.mult, ALU.mult), waits=[a3])
    lnb_done = dve(lambda v: v.engine_nop())
    yspill0 = ("ysl0", S.val("ysl0"))
    yspill1 = ("ysl1", S.val("ysl1"))
    p4_pe = ("pe", S.val("pe"))

    if "d_yscr" in dump_t:
        gdma("gdmp", lambda g: g.dma_start(out=dump_t["d_yscr"][:, :], in_=y_scr[:, :]),
             waits=[yspill0, yspill1])
    S.checkpoint("p4")

    att_stack.close()
    xs = sb("xs", [128, NCH * DM], F32)

    p5a_stack = contextlib.ExitStack()

    def sb5(name, shape, dtype):
        return p5a_stack.enter_context(nc.sbuf_tensor(name, shape, dtype))

    xa5 = sb5("xa5", [128, DM], F32)
    ya5 = sb5("ya5", [128, DM], F32)
    lntm5 = sb5("lntm5", [128, DM], BF16)

    ln2_tile_done = [None] * NCH
    prev_x = None
    prev_y = None
    for tc in range(NCH):
        xld = sdma("x5", lambda s, tc=tc: s.dma_start(
            out=xa5[:, :], in_=xc[tc * 128:(tc + 1) * 128, :]),
            waits=[p4_pe] + ([prev_x] if prev_x else []))
        yld = sdma("y5", lambda s, tc=tc: s.dma_start(
            out=ya5[:, :], in_=y_scr[tc * 128:(tc + 1) * 128, :]),
            waits=[yspill0, yspill1] + ([prev_y] if prev_y else []))
        ayn = act(lambda s, tc=tc: s.activation(
            ya5[:, :], ya5[:, :], AF.Identity,
            bias=nb_arr[:, tc:tc + 1], scale=rb_arr[:, tc:tc + 1]),
            waits=[yld, lnb_done])
        prev_y = ayn
        dx1 = dve(lambda v, tc=tc: v.tensor_add(
            xs[:, tc * DM:(tc + 1) * DM], xa5[:, :], ya5[:, :]),
            waits=[xld, ayn])
        prev_x = dx1
        stt = ln_stats(xs[:, tc * DM:(tc + 1) * DM], [dx1])
        ap5 = act(lambda s, tc=tc: s.activation(
            lntm5[:, :], xs[:, tc * DM:(tc + 1) * DM], AF.Identity,
            bias=st_sc[:, 5:6], scale=st_sc[:, 4:5]), waits=[stt])
        for b4 in range(4):
            ln2_tile_done[tc] = transpose4(
                lambda j, b4=b4: lntm5[:, (b4 * 4 + j) * 128:(b4 * 4 + j + 1) * 128],
                [(lnx_fm, (b4 * 4 + j) * TPC + tc * 128) for j in range(4)],
                [ap5, p3_done_pe, p4_pe])
    if "d_ln2x" in dump_t:
        sdma("dmp", lambda s: s.dma_start(out=dump_t["d_ln2x"][:, :], in_=lnx_fm[:, :]),
             waits=[ln2_tile_done[NCH - 1]])
    if "d_xs1" in dump_t:
        sdma("dmp", lambda s: s.dma_start(out=dump_t["d_xs1"][:, :], in_=xs[:, :]),
             waits=[prev_x])

    p5a_pe = ("pe", S.val("pe"))
    p5a_act = ("act", S.val("act"))
    p5a_dve = ("dve", S.val("dve"))
    S.checkpoint("p5a")

    p5a_stack.close()
    p5b_stack = contextlib.ExitStack()

    def sb5b(name, shape, dtype):
        return p5b_stack.enter_context(nc.sbuf_tensor(name, shape, dtype))

    fuse = sb5b("fuse", [128, NIT * 256], BF16)
    wg_b = [sb5b(f"wg_b{i}", [128, 4096], BF16) for i in range(2)]
    wu_b = [sb5b(f"wu_b{i}", [128, 4096], BF16) for i in range(2)]
    wd_b = [sb5b(f"wd_b{i}", [128, 1024], BF16) for i in range(2)]
    sg_st = [sb5b(f"sg_st{i}", [128, 256], BF16) for i in range(2)]
    uh_st = [sb5b(f"uh_st{i}", [128, 256], BF16) for i in range(2)]
    zst = [sb5b(f"zst{i}", [128, DM], F32) for i in range(2)]
    sdown_bc = sb5b("sdown_s", [128, DM], BF16)

    p5a_barrier = [p5a_pe, p5a_act, p5a_dve]
    sdown_ld = gdma("sd5", lambda g: g.dma_start(out=sdown_bc[:, :], in_=sdown_d[:, :]),
                    waits=p5a_barrier)
    wg_last = [p5a_barrier, p5a_barrier]
    wu_last = [p5a_barrier, p5a_barrier]
    wd_last = [p5a_barrier, p5a_barrier]
    set_war(ps_a, A_last)
    set_war(ps_b, y_ps_last[0])
    set_war(ps_c, O_last)
    set_war(ps_d, y_ps_last[1])
    g_pool = [ps_a, ps_c]
    u_pool = [ps_b, ps_d]
    zst_free = [None, None]
    out_marks = []
    fuse_dump_mark = [None]

    for g in range(4):
        lnc_rdy = ln2_tile_done[2 * g + 1]
        lnc_rdy0 = ln2_tile_done[2 * g]
        fuse_done = None
        fuse_by_isub = [fuse_dump_mark[0], fuse_dump_mark[0]]
        gmm = umm = None
        for ib in range(32):
            bi = ib % 2
            wgl = gdma(f"wg{bi}", lambda gp, ib=ib, bi=bi: gp.dma_start(
                out=wg_b[bi].ap().rearrange("p (q m i) -> p q m i", q=4, m=4),
                in_=wg5[ib].rearrange("q f m i -> f q m i")), waits=wg_last[bi])
            wul = gdma(f"wu{bi}", lambda gp, ib=ib, bi=bi: gp.dma_start(
                out=wu_b[bi].ap().rearrange("p (q m i) -> p q m i", q=4, m=4),
                in_=wu5[ib].rearrange("q f m i -> f q m i")), waits=wu_last[bi])
            for isub in range(2):
                it = ib * 2 + isub
                g_ps = g_pool[isub]
                u_ps = u_pool[isub]
                waits = [wgl, lnc_rdy, lnc_rdy0] + war(g_ps)
                for kt in range(KT):
                    col = (kt // 4) * 1024 + (kt % 4) * 256 + isub * 128
                    gmm = pe(lambda t, col=col, kt=kt, bi=bi, g=g, g_ps=g_ps:
                             t.matmul(
                                 g_ps[:, 0:256], wg_b[bi][:, col:col + 128],
                                 lnx_fm[:, kt * TPC + g * 256:
                                        kt * TPC + (g + 1) * 256],
                                 start=(kt == 0), stop=(kt == KT - 1)),
                             waits=waits if kt == 0 else [], inc=(kt == KT - 1))
                waits = [wul, lnc_rdy, lnc_rdy0] + war(u_ps)
                for kt in range(KT):
                    col = (kt // 4) * 1024 + (kt % 4) * 256 + isub * 128
                    umm = pe(lambda t, col=col, kt=kt, bi=bi, g=g, u_ps=u_ps:
                             t.matmul(
                                 u_ps[:, 0:256], wu_b[bi][:, col:col + 128],
                                 lnx_fm[:, kt * TPC + g * 256:
                                        kt * TPC + (g + 1) * 256],
                                 start=(kt == 0), stop=(kt == KT - 1)),
                             waits=waits if kt == 0 else [], inc=(kt == KT - 1))
                warf = [fuse_by_isub[isub]] if fuse_by_isub[isub] else []
                asg = act(lambda s, it=it, isub=isub, g_ps=g_ps: s.activation(
                    sg_st[isub][:, :], g_ps[:, 0:256], AF.Silu,
                    bias=0.0, scale=sg_t[:, it:it + 1]),
                    waits=[gmm, consts_s] + warf)
                set_war(g_ps, asg)
                auh = act(lambda s, it=it, isub=isub, u_ps=u_ps: s.activation(
                    uh_st[isub][:, :], u_ps[:, 0:256], AF.Identity,
                    bias=0.0, scale=su_t[:, it:it + 1]), waits=[umm] + warf)
                set_war(u_ps, auh)
                fuse_done = dve(lambda v, it=it, isub=isub: v.tensor_mul(
                    fuse[:, it * 256:(it + 1) * 256],
                    sg_st[isub][:, :], uh_st[isub][:, :]), waits=[asg, auh])
                fuse_by_isub[isub] = fuse_done
            wg_last[bi] = [gmm]
            wu_last[bi] = [umm]
        if g == 0 and "d_fuse" in dump_t:
            fdm = sdma("dmp", lambda s: s.dma_start(out=dump_t["d_fuse"][:, :],
                                                    in_=fuse[:, :]),
                       waits=[fuse_done])
            fuse_dump_mark[0] = fdm
        for oc in range(4):
            dps = {0: [ps_a, ps_c][oc % 2], 1: [ps_b, ps_d][oc % 2]}
            dmm = {0: None, 1: None}
            for itp in range(32):
                bi = itp % 2
                wdl = gdma(f"wd{bi}", lambda gp, oc=oc, itp=itp, bi=bi: gp.dma_start(
                    out=wd_b[bi].ap().rearrange("p (m o) -> p m o", m=2),
                    in_=wd5[oc, itp]), waits=wd_last[bi])
                for itm in range(2):
                    it = itp * 2 + itm
                    for tsub in range(2):
                        ps = dps[tsub]
                        waits = []
                        if itm == 0 and tsub == 0:
                            waits.append(wdl)
                        if it == 0:
                            waits.append(fuse_done)
                            waits += war(ps)
                        dmm[tsub] = pe(
                            lambda t, ps=ps, it=it, itm=itm, tsub=tsub, bi=bi:
                            t.matmul(
                                ps[:, 0:512],
                                fuse[:, it * 256 + tsub * 128:
                                     it * 256 + tsub * 128 + 128],
                                wd_b[bi][:, itm * 512:(itm + 1) * 512],
                                start=(it == 0), stop=(it == NIT - 1)),
                            waits=waits,
                            inc=(itm == 1 and tsub == 1) or (it == NIT - 1))
                wd_last[bi] = [dmm[1]]
            for tsub in range(2):
                ps = dps[tsub]
                waits = [dmm[tsub], sdown_ld]
                if zst_free[tsub] is not None:
                    waits.append(zst_free[tsub])
                dz = dve(lambda v, ps=ps, oc=oc, tsub=tsub: v.tensor_mul(
                    zst[tsub][:, oc * 512:(oc + 1) * 512],
                    ps[:, 0:512], sdown_bc[:, oc * 512:(oc + 1) * 512]),
                    waits=waits)
                set_war(ps, dz)
                act(lambda s, oc=oc, tsub=tsub: s.activation(
                    junk_a[:, 0:512], zst[tsub][:, oc * 512:(oc + 1) * 512],
                    AF.Identity, bias=0.0, scale=1.0,
                    accum_out=sp1[:, tsub * 4 + oc:tsub * 4 + oc + 1]),
                    waits=[dz])
                dve(lambda v, oc=oc, tsub=tsub: v.scalar_tensor_tensor(
                    junk_s[:, 0:512], zst[tsub][:, oc * 512:(oc + 1) * 512], 1.0,
                    zst[tsub][:, oc * 512:(oc + 1) * 512],
                    ALU.mult, ALU.mult,
                    accum_out=sp2[:, tsub * 4 + oc:tsub * 4 + oc + 1]),
                    waits=[dz])
        zdump = []
        if g == 0 and "d_z" in dump_t:
            zw = [("act", S.val("act")), ("dve", S.val("dve"))]
            for ts_ in range(2):
                zdump.append(sdma("dmp2", lambda s, ts_=ts_: s.dma_start(
                    out=dump_t["d_z"][:, ts_ * DM:(ts_ + 1) * DM],
                    in_=zst[ts_][:, :]), waits=zw))
        stats_in = [("act", S.val("act")), ("dve", S.val("dve"))]
        for tsub in range(2):
            tc = g * 2 + tsub
            dve(lambda v, tsub=tsub: v.reduce_sum(
                st_sc[:, 0:1], sp1[:, tsub * 4:tsub * 4 + 4], axis=AX.X),
                waits=stats_in)
            dve(lambda v, tsub=tsub: v.reduce_sum(
                st_sc[:, 1:2], sp2[:, tsub * 4:tsub * 4 + 4], axis=AX.X))
            dve(lambda v: v.tensor_scalar_mul(st_sc[:, 2:3], st_sc[:, 0:1],
                                              1.0 / DM))
            dve(lambda v: v.tensor_mul(st_sc[:, 6:7], st_sc[:, 2:3],
                                       st_sc[:, 2:3]))
            d4v = dve(lambda v: v.scalar_tensor_tensor(
                st_sc[:, 3:4], st_sc[:, 1:2], 1.0 / DM, st_sc[:, 6:7],
                ALU.mult, ALU.subtract))
            d4v = dve(lambda v: v.tensor_scalar_add(st_sc[:, 3:4], st_sc[:, 3:4],
                                                    EPS_LN))
            act(lambda s: s.activation(st_sc[:, 7:8], st_sc[:, 3:4], AF.Ln,
                                       bias=0.0, scale=1.0), waits=[d4v])
            a3 = act(lambda s: s.activation(st_sc[:, 4:5], st_sc[:, 7:8],
                                            AF.Exp, bias=0.0, scale=-0.5))
            d5v = dve(lambda v: v.scalar_tensor_tensor(
                st_sc[:, 5:6], st_sc[:, 2:3], -1.0, st_sc[:, 4:5],
                ALU.mult, ALU.mult), waits=[a3])
            azn = act(lambda s, tsub=tsub: s.activation(
                zst[tsub][:, :], zst[tsub][:, :], AF.Identity,
                bias=st_sc[:, 5:6], scale=st_sc[:, 4:5]),
                waits=[d5v] + (zdump[-1:] if zdump else []))
            dxf = dve(lambda v, tc=tc, tsub=tsub: v.tensor_add(
                xs[:, tc * DM:(tc + 1) * DM], xs[:, tc * DM:(tc + 1) * DM],
                zst[tsub][:, :]), waits=[azn])
            stm = sdma("out", lambda s, tc=tc: s.dma_start(
                out=yout[tc * 128:(tc + 1) * 128, :],
                in_=xs[:, tc * DM:(tc + 1) * DM]), waits=[dxf])
            zst_free[tsub] = dxf
            out_marks.append(stm)

    if stop_after is not None:
        lens, cnts = S.marks[stop_after]
        for e in ENGINES:
            S.recs[e] = S.recs[e][:lens[e]]
        S.cnt = cnts
    for sem in SY_DMA_SEMS:
        v = S.val(sem)
        if v > 0:
            S.emit("sy", lambda s, sem=sem, v=v: s.wait_ge(sems[sem], v))
    for sem in GP_DMA_SEMS:
        v = S.val(sem)
        if v > 0:
            S.emit("gp", lambda g, sem=sem, v=v: g.wait_ge(sems[sem], v))

    serialized = ("dve", "act")

    def replay(ename, eng):
        last_wait = {}
        nprev = 0
        for fn, waits, inc_sem, amt in S.recs[ename]:
            if ename in serialized and nprev > 0:
                eng.wait_ge(sems[ename], nprev)
            for (sname, v) in waits:
                if sname == ename and sname in COMPUTE_SEMS:
                    continue
                if last_wait.get(sname, -1) >= v:
                    continue
                eng.wait_ge(sems[sname], v)
                last_wait[sname] = v
            inst = fn(eng)
            if inc_sem is not None:
                if amt is None:
                    inst.then_inc(sems[inc_sem])
                else:
                    inst.then_inc(sems[inc_sem], amt)
            if inc_sem == ename:
                nprev += 1

    with nc.Block() as block:
        @block.tensor
        def _(eng):
            replay("pe", eng)

        @block.scalar
        def _(eng):
            replay("act", eng)

        @block.vector
        def _(eng):
            replay("dve", eng)

        @block.gpsimd
        def _(eng):
            replay("gp", eng)

        @block.sync
        def _(eng):
            replay("sy", eng)

    p5b_stack.close()
    stack.close()
    return nc


_NC = None
_NC_KEY = None


def _get_program(dumps=()):
    global _NC, _NC_KEY
    key = tuple(sorted(dumps))
    if _NC is None or _NC_KEY != key:
        _NC = build_program(dumps=key)
        _NC_KEY = key
    return _NC


def _prep_shared(inputs):
    d = {}
    for p, name in [("q", "L0_attn_Wq"), ("k", "L0_attn_Wk"),
                    ("v", "L0_attn_Wv"), ("o", "L0_attn_Wo")]:
        W = np.asarray(inputs[name])
        d[f"w4{p}"] = np.ascontiguousarray(
            W.reshape(KT, 128, 8, 256).transpose(2, 0, 1, 3))
    d["wg5"] = np.ascontiguousarray(
        np.asarray(inputs["L0_ffn_w_gate"]).reshape(4, 4, 128, 32, 256)
        .transpose(3, 0, 2, 1, 4))
    d["wu5"] = np.ascontiguousarray(
        np.asarray(inputs["L0_ffn_w_up"]).reshape(4, 4, 128, 32, 256)
        .transpose(3, 0, 2, 1, 4))
    d["wd5"] = np.ascontiguousarray(
        np.asarray(inputs["L0_ffn_w_down"]).reshape(32, 2, 128, 4, 512)
        .transpose(3, 0, 2, 1, 4))
    d["bqkv"] = np.ascontiguousarray(np.stack(
        [np.asarray(inputs["L0_attn_bq"]).reshape(KT, 128),
         np.asarray(inputs["L0_attn_bk"]).reshape(KT, 128),
         np.asarray(inputs["L0_attn_bv"]).reshape(KT, 128)],
        axis=0).reshape(48, 128).T)
    d["bo_bc"] = np.ascontiguousarray(
        np.broadcast_to(np.asarray(inputs["L0_attn_bo"]), (128, DM)))
    d["sdown_bc"] = np.ascontiguousarray(
        np.broadcast_to(np.asarray(inputs["L0_ffn_s_down"]), (128, DM)))
    d["sg_t"] = np.ascontiguousarray(
        np.asarray(inputs["L0_ffn_s_gate"]).reshape(NIT, 128).T)
    d["su_t"] = np.ascontiguousarray(
        np.asarray(inputs["L0_ffn_s_up"]).reshape(NIT, 128).T)
    d["id128"] = np.eye(128, dtype=np.float32)
    d["triu"] = np.triu(np.ones((128, 128), dtype=np.float32))
    return d


def make_in_maps(inputs):
    shared = _prep_shared(inputs)
    x = np.asarray(inputs["L0_attn_x"])
    in_maps = []
    for c in range(8):
        b, qi = c // 4, c % 4
        m = dict(shared)
        m["xc"] = np.ascontiguousarray(x[b, qi * TPC:(qi + 1) * TPC, :])
        pm = np.zeros((128, 4), dtype=np.float32)
        pm[:, :qi] = 1.0
        m["pmask"] = pm
        in_maps.append(m)
    return in_maps


def kernel(**inputs):
    nc = _get_program()
    in_maps = make_in_maps(inputs)
    res = run_bass_kernel_spmd(nc, in_maps, core_ids=list(range(8)))
    out = np.empty((2, 4096, DM), dtype=np.float32)
    for c in range(8):
        b, qi = c // 4, c % 4
        out[b, qi * TPC:(qi + 1) * TPC, :] = res.results[c]["yout"]
    return out


if __name__ == "__main__":
    import sys
    nc = build_program(dumps=tuple(sys.argv[1:]))
    print("build ok")


# version 05 (final)
# speedup vs baseline: 1.0848x; 1.0797x over previous
import contextlib
import numpy as np
import concourse.bass as bass
import concourse.mybir as mybir
from concourse.bass_utils import run_bass_kernel_spmd

F32 = mybir.dt.float32
BF16 = mybir.dt.bfloat16
I8 = mybir.dt.int8
AF = mybir.ActivationFunctionType
ALU = mybir.AluOpType
AX = mybir.AxisListType

TPC = 1024
NCH = 8
H = 16
DM = 2048
KT = 16
NIT = 64
EPS_LN = 1e-5
EPS_Z = 1e-6
VST = 130

ENGINES = ("pe", "act", "dve", "gp", "sy")
COMPUTE_SEMS = {"pe", "act", "dve"}


class Sched:

    def __init__(self):
        self.recs = {e: [] for e in ENGINES}
        self.cnt = {}
        self.marks = {}

    def checkpoint(self, name):
        self.marks[name] = ({e: len(self.recs[e]) for e in ENGINES},
                            dict(self.cnt))

    def emit(self, eng, fn, waits=(), inc=None, amt=1):
        waits = [w for w in waits if w is not None]
        waits = [(s, v) for (s, v) in waits if v > 0]
        mark = None
        if inc is not None:
            self.cnt[inc] = self.cnt.get(inc, 0) + (amt if amt else 1)
            mark = (inc, self.cnt[inc])
        self.recs[eng].append((fn, waits, inc if mark else None, amt))
        return mark

    def val(self, sem):
        return self.cnt.get(sem, 0)


def build_program(dumps=(), stop_after=None):
    nc = bass.Bass("TRN2", target_bir_lowering=False)

    xc = nc.dram_tensor("xc", [TPC, DM], F32, kind="ExternalInput").ap()
    w4 = {p: nc.dram_tensor(f"w4{p}", [8, KT, 128, 256], F32, kind="ExternalInput").ap()
          for p in "qkvo"}
    wg5 = nc.dram_tensor("wg5", [32, 4, 128, 4, 256], I8, kind="ExternalInput").ap()
    wu5 = nc.dram_tensor("wu5", [32, 4, 128, 4, 256], I8, kind="ExternalInput").ap()
    wd5 = nc.dram_tensor("wd5", [4, 32, 128, 2, 512], I8, kind="ExternalInput").ap()
    bqkv_d = nc.dram_tensor("bqkv", [128, 48], F32, kind="ExternalInput").ap()
    bo_bc_d = nc.dram_tensor("bo_bc", [128, DM], F32, kind="ExternalInput").ap()
    sdown_d = nc.dram_tensor("sdown_bc", [128, DM], F32, kind="ExternalInput").ap()
    sg_d = nc.dram_tensor("sg_t", [128, NIT], F32, kind="ExternalInput").ap()
    su_d = nc.dram_tensor("su_t", [128, NIT], F32, kind="ExternalInput").ap()
    pm_d = nc.dram_tensor("pmask", [128, 4], F32, kind="ExternalInput").ap()
    id_d = nc.dram_tensor("id128", [128, 128], F32, kind="ExternalInput").ap()
    tri_d = nc.dram_tensor("triu", [128, 128], F32, kind="ExternalInput").ap()
    tri4_d = nc.dram_tensor("triu4d", [128, 512], F32, kind="ExternalInput").ap()

    yout = nc.dram_tensor("yout", [TPC, DM], F32, kind="ExternalOutput").ap()
    dump_specs = {
        "d_lnx": ([128, KT * TPC], BF16), "d_kftm": ([128, NCH * H * 128], BF16),
        "d_vtm": ([128, NCH * H * VST], BF16), "d_kvtot": ([128, H * VST], F32),
        "d_state0": ([128, H * VST], F32), "d_qf": ([128, H * TPC], BF16),
        "d_yscr": ([TPC, DM], F32), "d_ln2x": ([128, KT * TPC], BF16),
        "d_xs1": ([128, NCH * DM], F32),
        "d_xa": ([128, DM], F32), "d_stsc": ([128, 8], F32),
        "d_fuse": ([128, NIT * 256], BF16), "d_z": ([128, 2 * DM], F32),
        "d_lntm": ([128, DM], BF16), "d_sp": ([128, 32], F32),
    }
    dump_t = {n: nc.dram_tensor(n, s, d, kind="ExternalOutput").ap()
              for n, (s, d) in dump_specs.items() if n in dumps}

    y_scr = nc.dram_tensor("y_scr", [TPC, DM], F32, kind="Internal").ap()
    cc_in = nc.dram_tensor("cc_in", [128, H * VST], F32, kind="Internal")
    cc_out = nc.dram_tensor("cc_out", [4 * 128, H * VST], F32, kind="Internal")

    S = Sched()
    stack = contextlib.ExitStack()

    def sb(name, shape, dtype):
        return stack.enter_context(nc.sbuf_tensor(name, shape, dtype))

    lnx_fm = sb("lnx_fm", [128, KT * TPC], BF16)
    ident = sb("ident", [128, 128], BF16)
    triu = sb("triu_s", [128, 128], BF16)
    bqkv = sb("bqkv_s", [128, 48], F32)
    bo_bc = sb("bo_bc_s", [128, DM], BF16)
    sg_t = sb("sg_s", [128, NIT], F32)
    su_t = sb("su_s", [128, NIT], F32)
    pmask = sb("pmask_s", [128, 4], F32)
    kv_state = sb("kv_state", [128, H * VST], F32)
    state_bf = sb("state_bf", [128, H * VST], BF16)
    mb_arr = sb("mb_arr", [128, NCH], F32)
    rb_arr = sb("rb_arr", [128, NCH], F32)
    nb_arr = sb("nb_arr", [128, NCH], F32)
    sp1 = sb("sp1", [128, 32], F32)
    sp2 = sb("sp2", [128, 32], F32)
    st_sc = sb("st_sc", [128, 8], F32)
    rz_sc = sb("rz_sc", [128, 2], F32)
    junk_s = sb("junk_s", [128, 512], BF16)
    junk_a = sb("junk_a", [128, 512], BF16)

    GP_DMA_SEMS = ["cg", "w0", "w1", "wg0", "wg1", "wu0", "wu1",
                   "wd0", "wd1", "sd5", "gdmp"]
    SY_DMA_SEMS = ["cs", "xa", "dmp", "dmp2", "ccin", "gath", "ysl0", "ysl1",
                   "x5", "y5", "out"]
    sem_names = ["pe", "act", "dve", "cc"] + GP_DMA_SEMS + SY_DMA_SEMS
    sems = {n: stack.enter_context(nc.semaphore(f"s_{n}")) for n in sem_names}

    def gdma(sem, fn, waits=()):
        return S.emit("gp", fn, waits=waits, inc=sem, amt=16)

    def sdma(sem, fn, waits=()):
        return S.emit("sy", fn, waits=waits, inc=sem, amt=16)

    def pe(fn, waits=(), inc=True):
        return S.emit("pe", fn, waits=waits, inc="pe" if inc else None)

    def act(fn, waits=(), inc=True):
        return S.emit("act", fn, waits=waits, inc="act")

    def dve(fn, waits=(), inc=True):
        return S.emit("dve", fn, waits=waits, inc="dve")

    att_stack = contextlib.ExitStack()

    def sba(name, shape, dtype):
        return att_stack.enter_context(nc.sbuf_tensor(name, shape, dtype))

    qf_fm = sba("qf_fm", [128, H * TPC], BF16)
    kf_tm = sba("kf_tm", [128, NCH * H * 128], BF16)
    v_tm = sba("v_tm", [128, NCH * H * VST], BF16)
    out_fm = sba("out_fm", [128, H * 4 * 128], BF16)
    kv_total = sba("kv_total", [128, H * VST], F32)
    wbuf = [sba(f"wbuf{i}", [128, KT * 256], BF16) for i in range(2)]
    r_st = [sba(f"r_st{i}", [128, 512], BF16) for i in range(2)]
    fm_st = [sba(f"fm_st{i}", [128, 512], BF16) for i in range(2)]
    triu4 = sba("triu4", [128, 512], BF16)
    ones_col = sba("ones_col", [128, 1], BF16)
    rz8 = sba("rz8", [128, 16], F32)
    xa_p1 = sba("xa_p1", [128, DM], F32)
    sc_bf = sba("sc_bf", [128, DM], BF16)
    lntm_st = sc_bf
    ysl_st = [sba(f"ysl{i}", [128, 256], F32) for i in range(2)]

    ps_a = stack.enter_context(nc.psum_tensor("ps_a", [128, 512], F32))
    ps_b = stack.enter_context(nc.psum_tensor("ps_b", [128, 512], F32))
    ps_c = stack.enter_context(nc.psum_tensor("ps_c", [128, 512], F32))
    ps_d = stack.enter_context(nc.psum_tensor("ps_d", [128, 512], F32))
    ps_t1 = stack.enter_context(nc.psum_tensor("ps_t1", [128, 1024], BF16))
    ps_t2 = stack.enter_context(nc.psum_tensor("ps_t2", [128, 1024], BF16))
    ps_s1 = stack.enter_context(nc.psum_tensor("ps_s1", [128, 512], F32))
    ps_s2 = stack.enter_context(nc.psum_tensor("ps_s2", [128, 512], F32))

    ps_war = {}

    def war(ps):
        m = ps_war.get(ps.name)
        return [m] if m else []

    def set_war(ps, mark):
        ps_war[ps.name] = mark

    gdma("cg", lambda g: g.dma_start(out=ident[:, :], in_=id_d[:, :]))
    gdma("cg", lambda g: g.dma_start(out=triu[:, :], in_=tri_d[:, :]))
    gdma("cg", lambda g: g.dma_start(out=triu4[:, :], in_=tri4_d[:, :]))
    gdma("cg", lambda g: g.dma_start(out=bo_bc[:, :], in_=bo_bc_d[:, :]))
    consts_g = ("cg", S.val("cg"))
    sdma("cs", lambda s: s.dma_start(out=bqkv[:, :], in_=bqkv_d[:, :]))
    sdma("cs", lambda s: s.dma_start(out=sg_t[:, :], in_=sg_d[:, :]))
    sdma("cs", lambda s: s.dma_start(out=su_t[:, :], in_=su_d[:, :]))
    sdma("cs", lambda s: s.dma_start(out=pmask[:, :], in_=pm_d[:, :]))
    consts_s = ("cs", S.val("cs"))
    ones_m = dve(lambda v: v.memset(
        v_tm.ap().rearrange("p (nh s) -> p nh s", s=VST)[:, :, 128:VST], 1.0))
    kvt_z = dve(lambda v: v.memset(kv_total[:, :], 0.0))
    onec_m = dve(lambda v: v.memset(ones_col[:, :], 1.0))


    def ln_stats(src_full, waits_src):
        a1 = d1 = None
        for q in range(4):
            sl = src_full[:, q * 512:(q + 1) * 512]
            a1 = act(lambda s, sl=sl, q=q: s.activation(
                junk_a[:, 0:512], sl, AF.Identity, bias=0.0, scale=1.0,
                accum_out=sp1[:, 24 + q:25 + q]),
                waits=waits_src if q == 0 else [])
            d1 = dve(lambda v, sl=sl, q=q: v.scalar_tensor_tensor(
                junk_s[:, 0:512], sl, 1.0, sl, ALU.mult, ALU.mult,
                accum_out=sp2[:, 24 + q:25 + q]),
                waits=waits_src if q == 0 else [])
        dve(lambda v: v.reduce_sum(st_sc[:, 0:1], sp1[:, 24:28], axis=AX.X),
            waits=[a1, d1])
        dve(lambda v: v.reduce_sum(st_sc[:, 1:2], sp2[:, 24:28], axis=AX.X))
        dve(lambda v: v.tensor_scalar_mul(st_sc[:, 2:3], st_sc[:, 0:1], 1.0 / DM))
        dve(lambda v: v.tensor_mul(st_sc[:, 6:7], st_sc[:, 2:3], st_sc[:, 2:3]))
        d4 = dve(lambda v: v.scalar_tensor_tensor(
            st_sc[:, 3:4], st_sc[:, 1:2], 1.0 / DM, st_sc[:, 6:7],
            ALU.mult, ALU.subtract))
        d4 = dve(lambda v: v.tensor_scalar_add(st_sc[:, 3:4], st_sc[:, 3:4],
                                               EPS_LN))
        act(lambda s: s.activation(st_sc[:, 7:8], st_sc[:, 3:4], AF.Ln,
                                   bias=0.0, scale=1.0), waits=[d4])
        a3 = act(lambda s: s.activation(st_sc[:, 4:5], st_sc[:, 7:8], AF.Exp,
                                        bias=0.0, scale=-0.5))
        d5 = dve(lambda v: v.scalar_tensor_tensor(
            st_sc[:, 5:6], st_sc[:, 2:3], -1.0, st_sc[:, 4:5],
            ALU.mult, ALU.mult), waits=[a3])
        return d5

    tpose_ps = [ps_t1, ps_t2]
    tpose_last = [None, None]
    tp_rot = [0]

    def transpose4(src_ap_fn, dst_list, waits_src):
        pi = tp_rot[0] % 2
        tp_rot[0] += 1
        psT = tpose_ps[pi]
        waits = list(waits_src) + [consts_g]
        if tpose_last[pi] is not None:
            waits.append(tpose_last[pi])
        tr = None
        for j in range(4):
            tr = pe(lambda t, j=j, psT=psT: t.transpose(
                psT[:, j * 128:(j + 1) * 128], src_ap_fn(j), ident[:, :]),
                waits=waits if j == 0 else [])
        ev = None
        for j in range(4):
            buf, col = dst_list[j]
            ev = act(lambda s, j=j, psT=psT, buf=buf, col=col: s.activation(
                buf[:, col:col + 128], psT[:, j * 128:(j + 1) * 128],
                AF.Identity, bias=0.0, scale=1.0), waits=[tr] if j == 0 else [])
        tpose_last[pi] = ev
        return ev

    p1_last = None
    prev_consumed = None
    for tc in range(NCH):
        ld = sdma("xa", lambda s, tc=tc: s.dma_start(
            out=xa_p1[:, :], in_=xc[tc * 128:(tc + 1) * 128, :]),
            waits=[prev_consumed] if prev_consumed else [])
        stt = ln_stats(xa_p1, [ld])
        if tc == 0 and "d_xa" in dump_t:
            sdma("dmp", lambda s: s.dma_start(out=dump_t["d_xa"][:, :], in_=xa_p1[:, :]),
                 waits=[stt])
        if tc == 0 and "d_stsc" in dump_t:
            sdma("dmp", lambda s: s.dma_start(out=dump_t["d_stsc"][:, :], in_=st_sc[:, :]),
                 waits=[stt])
        if tc == 0 and "d_sp" in dump_t:
            sdma("dmp", lambda s: s.dma_start(out=dump_t["d_sp"][:, :], in_=sp1[:, :]),
                 waits=[stt])
        ap = act(lambda s: s.activation(
            lntm_st[:, :], xa_p1[:, :], AF.Identity,
            bias=st_sc[:, 5:6], scale=st_sc[:, 4:5]), waits=[stt])
        if tc == 0 and "d_lntm" in dump_t:
            sdma("dmp", lambda s: s.dma_start(out=dump_t["d_lntm"][:, :], in_=lntm_st[:, :]),
                 waits=[ap])
        prev_consumed = ap
        for b4 in range(4):
            p1_last = transpose4(
                lambda j, b4=b4: lntm_st[:, (b4 * 4 + j) * 128:(b4 * 4 + j + 1) * 128],
                [(lnx_fm, (b4 * 4 + j) * TPC + tc * 128) for j in range(4)],
                [ap])
    lnx_ready = p1_last

    if "d_lnx" in dump_t:
        sdma("dmp", lambda s: s.dma_start(out=dump_t["d_lnx"][:, :], in_=lnx_fm[:, :]),
             waits=[lnx_ready])
    S.checkpoint("p1")

    wlast_use = [None, None]
    proj_ps = [ps_a, ps_b]
    proj_ps_last = [None, None]

    def stream_attn_w(p, ob, bi):
        waits = []
        if wlast_use[bi] is not None:
            waits.append(wlast_use[bi])
        return gdma(f"w{bi}", lambda g, p=p, ob=ob, bi=bi: g.dma_start(
            out=wbuf[bi].ap().rearrange("p (kt o) -> p kt o", kt=KT),
            in_=w4[p][ob].rearrange("kt f o -> f kt o")), waits=waits)

    elu_n = [0]

    def elu_evac(ps, mm, bcol, st_i):
        first = elu_n[0] == 0
        elu_n[0] += 1
        a1 = act(lambda s: s.activation(
            r_st[st_i][:, :], ps[:, 0:512], AF.Relu,
            bias=bqkv[:, bcol:bcol + 1], scale=1.0), waits=[mm, consts_s])
        if first:
            S.checkpoint("e1")
        d1 = dve(lambda v: v.tensor_scalar(
            fm_st[st_i][:, :], ps[:, 0:512], bqkv[:, bcol:bcol + 1], 0.0,
            ALU.add, ALU.min), waits=[mm, consts_s, a1])
        if first:
            S.checkpoint("e2")
        a2 = act(lambda s: s.activation(
            fm_st[st_i][:, :], fm_st[st_i][:, :], AF.Exp, bias=0.0, scale=1.0),
            waits=[d1])
        if first:
            S.checkpoint("e3")
        d2 = dve(lambda v: v.tensor_add(
            fm_st[st_i][:, :], r_st[st_i][:, :], fm_st[st_i][:, :]),
            waits=[a1, a2])
        if first:
            S.checkpoint("e4")
        return d2

    def proj_pass(p, consume):
        for ob in range(8):
            if p == "k" and ob == 1:
                S.checkpoint("p2k0")
            bi = ob % 2
            wld = stream_attn_w(p, ob, bi)
            for th in range(2):
                for osub in range(2):
                    h = ob * 2 + osub
                    k = (th * 2 + osub) % 2
                    ps = proj_ps[k]
                    waits = [wld, lnx_ready]
                    if proj_ps_last[k] is not None:
                        waits.append(proj_ps_last[k])
                    mm = None
                    for kt in range(KT):
                        mm = pe(lambda t, ps=ps, kt=kt, bi=bi, osub=osub, th=th:
                                t.matmul(
                                    ps[:, 0:512],
                                    wbuf[bi][:, kt * 256 + osub * 128:
                                             kt * 256 + (osub + 1) * 128],
                                    lnx_fm[:, kt * TPC + th * 512:
                                           kt * TPC + (th + 1) * 512],
                                    start=(kt == 0), stop=(kt == KT - 1)),
                                waits=waits if kt == 0 else [],
                                inc=(kt == KT - 1))
                    wlast_use[bi] = mm
                    if p == "k" and ob == 0 and th == 0 and osub == 0:
                        S.checkpoint("p2m")
                    proj_ps_last[k] = consume(h, th, ps, mm, k)
                    if p == "k" and ob == 0 and th == 0 and osub == 0:
                        S.checkpoint("p2u")

    def consume_k(h, th, ps, mm, st_i):
        rdy = elu_evac(ps, mm, 16 + h, st_i)
        transpose4(
            lambda j: fm_st[st_i][:, j * 128:(j + 1) * 128],
            [(kf_tm, ((th * 4 + j) * H + h) * 128) for j in range(4)],
            [rdy])
        return rdy

    def consume_v(h, th, ps, mm, st_i):
        rdy = act(lambda s: s.activation(
            fm_st[st_i][:, :], ps[:, 0:512], AF.Identity,
            bias=bqkv[:, 32 + h:32 + h + 1], scale=1.0), waits=[mm, consts_s])
        transpose4(
            lambda j: fm_st[st_i][:, j * 128:(j + 1) * 128],
            [(v_tm, ((th * 4 + j) * H + h) * VST) for j in range(4)],
            [rdy])
        return rdy

    proj_pass("k", consume_k)
    S.checkpoint("p2k")
    proj_pass("v", consume_v)
    p2_done_act = ("act", S.val("act"))

    if "d_kftm" in dump_t:
        sdma("dmp", lambda s: s.dma_start(out=dump_t["d_kftm"][:, :], in_=kf_tm[:, :]),
             waits=[p2_done_act])
    if "d_vtm" in dump_t:
        sdma("dmp", lambda s: s.dma_start(out=dump_t["d_vtm"][:, :], in_=v_tm[:, :]),
             waits=[p2_done_act, ones_m])
    S.checkpoint("p2")

    tot_ps = [ps_s1, ps_s2]
    tot_last = [None, None]
    tot_ev = None
    for h in range(H):
        k = h % 2
        kps = tot_ps[k]
        waits = [p2_done_act, ones_m]
        if tot_last[k] is not None:
            waits.append(tot_last[k])
        mm = None
        for n in range(NCH):
            mm = pe(lambda t, kps=kps, n=n, h=h: t.matmul(
                kps[:, 0:129],
                kf_tm[:, (n * H + h) * 128:(n * H + h) * 128 + 128],
                v_tm[:, (n * H + h) * VST:(n * H + h) * VST + 129],
                start=(n == 0), stop=(n == NCH - 1)),
                waits=waits if n == 0 else [], inc=(n == NCH - 1))
        tot_ev = dve(lambda v, kps=kps, h=h: v.tensor_copy(
            kv_total[:, h * VST:h * VST + 129], kps[:, 0:129]), waits=[mm])
        tot_last[k] = tot_ev
    totals_done = tot_ev
    if "d_kvtot" in dump_t:
        sdma("dmp", lambda s: s.dma_start(out=dump_t["d_kvtot"][:, :], in_=kv_total[:, :]),
             waits=[totals_done])

    st_in = sdma("ccin", lambda s: s.dma_start(out=cc_in.ap()[:, :], in_=kv_total[:, :]),
                 waits=[totals_done])
    S.emit("gp", lambda g: g.collective_compute(
        "AllGather", ALU.bypass,
        replica_groups=[[0, 1, 2, 3], [4, 5, 6, 7]],
        ins=[cc_in.ap().opt()], outs=[cc_out.ap().opt()]),
        waits=[st_in], inc="cc", amt=None)
    cc_done = ("cc", 1)
    S.checkpoint("p2b")

    qf_last = [None]

    def consume_q(h, th, ps, mm, st_i):
        rdy = elu_evac(ps, mm, h, st_i)
        qf_last[0] = dve(lambda v, h=h, th=th, st_i=st_i: v.tensor_copy(
            qf_fm[:, h * TPC + th * 512: h * TPC + (th + 1) * 512],
            fm_st[st_i][:, :]), waits=[rdy])
        return qf_last[0]

    proj_pass("q", consume_q)
    qf_done = qf_last[0]
    p3_done_pe = ("pe", S.val("pe"))

    if "d_qf" in dump_t:
        sdma("dmp", lambda s: s.dma_start(out=dump_t["d_qf"][:, :], in_=qf_fm[:, :]),
             waits=[qf_done])
    S.checkpoint("p3")

    pf = dve(lambda v: v.memset(kv_state[:, :], 0.0), waits=[cc_done, totals_done])
    for r in range(4):
        gl = sdma("gath", lambda s, r=r: s.dma_start(
            out=kv_total[:, :], in_=cc_out.ap()[r * 128:(r + 1) * 128, :]),
            waits=[cc_done] + ([pf] if r > 0 else []))
        pf = dve(lambda v, r=r: v.scalar_tensor_tensor(
            kv_state[:, :], kv_total[:, :], pmask[:, r:r + 1], kv_state[:, :],
            ALU.mult, ALU.add), waits=[gl, consts_s])
    st_f32 = pf
    st_upd = act(lambda s: s.activation(state_bf[:, :], kv_state[:, :],
                                        AF.Identity, bias=0.0, scale=1.0),
                 waits=[pf])
    if "d_state0" in dump_t:
        sdma("dmp", lambda s: s.dma_start(out=dump_t["d_state0"][:, :], in_=kv_state[:, :]),
             waits=[pf])
    S.checkpoint("p3b")

    oT_last = tpose_last[1]
    set_war(ps_t1, tpose_last[0])
    set_war(ps_t2, tpose_last[1])
    set_war(ps_a, proj_ps_last[0])
    set_war(ps_b, proj_ps_last[1])
    set_war(ps_s1, tot_last[0])
    set_war(ps_s2, tot_last[1])
    st_bf_war = [st_upd, st_upd]
    ysl_dma = [None, None]

    for grp in range(2):
        for n4 in range(4):
            n = grp * 4 + n4
            for hb in range(2):
                h0 = hb * 8
                kvv = kv_state.ap().rearrange("p (h s) -> p h s", s=VST)
                sbv = state_bf.ap().rearrange("p (h s) -> p h s", s=VST)
                waits = [p2_done_act, consts_g] + war(ps_t1)
                t1 = None
                for j in range(8):
                    h = h0 + j
                    t1 = pe(lambda t, n=n, h=h, j=j: t.transpose(
                        ps_t1[:, j * 128:(j + 1) * 128],
                        kf_tm[:, (n * H + h) * 128:(n * H + h) * 128 + 128],
                        ident[:, :]), waits=waits if j == 0 else [])
                e1 = act(lambda s: s.activation(
                    sc_bf[:, 1024:2048], ps_t1[:, :], AF.Identity, bias=0.0, scale=1.0),
                    waits=[t1])
                set_war(ps_t1, e1)
                AB = [ps_a, ps_b]
                waits = [e1, qf_done] + war(ps_a) + war(ps_b)
                m1 = None
                for j in range(8):
                    h = h0 + j
                    pj = AB[j // 4]
                    cj = (j % 4) * 128
                    m1 = pe(lambda t, n=n, h=h, j=j, pj=pj, cj=cj: t.matmul(
                        pj[:, cj:cj + 128], sc_bf[:, 1024 + j * 128:1024 + (j + 1) * 128],
                        qf_fm[:, h * TPC + n * 128: h * TPC + (n + 1) * 128],
                        start=True, stop=True), waits=waits if j == 0 else [])
                d1 = dve(lambda v: v.tensor_mul(
                    sc_bf[:, 0:512], ps_a[:, 0:512], triu4[:, :]),
                    waits=[m1, consts_g])
                d1 = dve(lambda v: v.tensor_mul(
                    sc_bf[:, 512:1024], ps_b[:, 0:512], triu4[:, :]))
                pZ = ps_s1
                CD = [ps_c, ps_d]
                waits = [d1, st_upd] + war(ps_c) + war(ps_d) + war(pZ)
                m_last = None
                for j in range(8):
                    h = h0 + j
                    pj = AB[j // 4]
                    cj = (j % 4) * 128
                    base = (n * H + h)
                    first = (j == 0)
                    pe(lambda t, j=j, pj=pj, cj=cj, base=base: t.matmul(
                        pj[:, cj:cj + 128], sc_bf[:, j * 128:(j + 1) * 128],
                        v_tm[:, base * VST:base * VST + 128],
                        start=True, stop=False), waits=waits if first else [])
                    pe(lambda t, h=h, n=n, pj=pj, cj=cj: t.matmul(
                        pj[:, cj:cj + 128],
                        qf_fm[:, h * TPC + n * 128: h * TPC + (n + 1) * 128],
                        state_bf[:, h * VST:h * VST + 128],
                        start=False, stop=True))
                    pe(lambda t, j=j: t.matmul(
                        pZ[:, j:j + 1], sc_bf[:, j * 128:(j + 1) * 128],
                        ones_col[:, :], start=True, stop=False))
                    pe(lambda t, h=h, n=n, j=j: t.matmul(
                        pZ[:, j:j + 1],
                        qf_fm[:, h * TPC + n * 128: h * TPC + (n + 1) * 128],
                        state_bf[:, h * VST + 128:h * VST + 129],
                        start=False, stop=True))
                    pe(lambda t, j=j, base=base: t.matmul(
                        CD[j // 4][:, (j % 4) * 128:(j % 4) * 128 + 128],
                        kf_tm[:, base * 128:base * 128 + 128],
                        v_tm[:, base * VST:base * VST + 128],
                        start=True, stop=True))
                    m_last = pe(lambda t, j=j, base=base: t.matmul(
                        pZ[:, 8 + j:9 + j],
                        kf_tm[:, base * 128:base * 128 + 128],
                        ones_col[:, :], start=True, stop=True))
                dve(lambda v: v.tensor_scalar_add(rz8[:, 0:8], pZ[:, 0:8], EPS_Z),
                    waits=[m_last])
                d3 = dve(lambda v: v.reciprocal(rz8[:, 8:16], rz8[:, 0:8]))
                set_war(pZ, d3)
                a4 = None
                for j in range(8):
                    pj = AB[j // 4]
                    cj = (j % 4) * 128
                    a4 = act(lambda s, j=j, pj=pj, cj=cj: s.activation(
                        sc_bf[:, j * 128:(j + 1) * 128], pj[:, cj:cj + 128],
                        AF.Identity, bias=0.0, scale=rz8[:, 8 + j:9 + j]),
                        waits=[d3] if j == 0 else [])
                set_war(ps_a, a4)
                set_war(ps_b, a4)
                waits = [a4] + war(ps_t2)
                t2 = None
                for j in range(8):
                    t2 = pe(lambda t, j=j: t.transpose(
                        ps_t2[:, j * 128:(j + 1) * 128],
                        sc_bf[:, j * 128:(j + 1) * 128], ident[:, :]),
                        waits=waits if j == 0 else [])
                e2 = act(lambda s, n4=n4, h0=h0: s.activation(
                    out_fm.ap().rearrange("p (h f c) -> p h f c", f=4, c=128)
                    [:, h0:h0 + 8, n4, :],
                    ps_t2.ap().rearrange("p (j c) -> p j c", c=128),
                    AF.Identity, bias=0.0, scale=1.0), waits=[t2])
                set_war(ps_t2, e2)
                oT_last = e2
                d5 = dve(lambda v, h0=h0: v.tensor_add(
                    kvv[:, h0:h0 + 4, 0:128],
                    kvv[:, h0:h0 + 4, 0:128],
                    ps_c.ap().rearrange("p (j c) -> p j c", c=128)),
                    waits=[m_last, st_bf_war[hb]])
                d5 = dve(lambda v, h0=h0: v.tensor_add(
                    kvv[:, h0 + 4:h0 + 8, 0:128],
                    kvv[:, h0 + 4:h0 + 8, 0:128],
                    ps_d.ap().rearrange("p (j c) -> p j c", c=128)))
                d5 = dve(lambda v, h0=h0: v.tensor_add(
                    kvv[:, h0:h0 + 8, 128:129],
                    kvv[:, h0:h0 + 8, 128:129],
                    pZ.ap().rearrange("p (j o) -> p j o", o=1)[:, 8:16, :]))
                set_war(ps_c, d5)
                set_war(ps_d, d5)
                st_upd = act(lambda s, h0=h0: s.activation(
                    sbv[:, h0:h0 + 8, 0:129], kvv[:, h0:h0 + 8, 0:129],
                    AF.Identity, bias=0.0, scale=1.0), waits=[d5])
                st_bf_war[hb] = st_upd
        grp_out_ready = oT_last
        for ob in range(8):
            bi = ob % 2
            waits = [grp_out_ready]
            if wlast_use[bi] is not None:
                waits.append(wlast_use[bi])
            wld = gdma(f"w{bi}", lambda g, ob=ob, bi=bi: g.dma_start(
                out=wbuf[bi].ap().rearrange("p (kt o) -> p kt o", kt=KT),
                in_=w4["o"][ob].rearrange("kt f o -> f kt o")), waits=waits)
            for n4 in range(4):
                n = grp * 4 + n4
                ps = [ps_a, ps_b][n4 % 2]
                waits = [wld, grp_out_ready, consts_g] + war(ps)
                mm = None
                for kt in range(KT):
                    mm = pe(lambda t, ps=ps, kt=kt, n4=n4, bi=bi: t.matmul(
                        ps[:, 0:256],
                        out_fm[:, (kt * 4 + n4) * 128:(kt * 4 + n4) * 128 + 128],
                        wbuf[bi][:, kt * 256:(kt + 1) * 256],
                        start=(kt == 0), stop=(kt == KT - 1)),
                        waits=waits if kt == 0 else [], inc=(kt == KT - 1))
                wlast_use[bi] = mm
                yi = (ob * 4 + n4) % 2
                waits = [mm]
                if ysl_dma[yi] is not None:
                    waits.append(ysl_dma[yi])
                d6 = dve(lambda v, ps=ps, ob=ob, yi=yi: v.tensor_add(
                    ysl_st[yi][:, :], ps[:, 0:256],
                    bo_bc[:, ob * 256:(ob + 1) * 256]), waits=waits)
                set_war(ps, d6)
                a7 = act(lambda s, ob=ob, n4=n4, yi=yi: s.activation(
                    junk_a[:, 0:256], ysl_st[yi][:, :],
                    AF.Identity, bias=0.0, scale=1.0,
                    accum_out=sp1[:, n4 * 8 + ob:n4 * 8 + ob + 1]), waits=[d6])
                d7 = dve(lambda v, ob=ob, n4=n4, yi=yi: v.scalar_tensor_tensor(
                    junk_s[:, 0:256], ysl_st[yi][:, :], 1.0, ysl_st[yi][:, :],
                    ALU.mult, ALU.mult,
                    accum_out=sp2[:, n4 * 8 + ob:n4 * 8 + ob + 1]), waits=[d6])
                ysl_dma[yi] = sdma(f"ysl{yi}", lambda s, n=n, ob=ob, yi=yi: s.dma_start(
                    out=y_scr[n * 128:(n + 1) * 128, ob * 256:(ob + 1) * 256],
                    in_=ysl_st[yi][:, :]), waits=[d7, a7])
        stats_in = [("act", S.val("act")), ("dve", S.val("dve"))]
        for n4 in range(4):
            n = grp * 4 + n4
            dve(lambda v, n4=n4: v.reduce_sum(
                st_sc[:, 0:1], sp1[:, n4 * 8:n4 * 8 + 8], axis=AX.X),
                waits=stats_in)
            dve(lambda v, n4=n4: v.reduce_sum(
                st_sc[:, 1:2], sp2[:, n4 * 8:n4 * 8 + 8], axis=AX.X))
            dve(lambda v, n=n: v.tensor_scalar_mul(
                mb_arr[:, n:n + 1], st_sc[:, 0:1], 1.0 / DM))
            dve(lambda v, n=n: v.tensor_mul(
                st_sc[:, 6:7], mb_arr[:, n:n + 1], mb_arr[:, n:n + 1]))
            d4v = dve(lambda v: v.scalar_tensor_tensor(
                st_sc[:, 3:4], st_sc[:, 1:2], 1.0 / DM, st_sc[:, 6:7],
                ALU.mult, ALU.subtract))
            d4v = dve(lambda v: v.tensor_scalar_add(st_sc[:, 3:4], st_sc[:, 3:4],
                                                    EPS_LN))
            act(lambda s: s.activation(st_sc[:, 7:8], st_sc[:, 3:4], AF.Ln,
                                       bias=0.0, scale=1.0), waits=[d4v])
            a3 = act(lambda s, n=n: s.activation(
                rb_arr[:, n:n + 1], st_sc[:, 7:8], AF.Exp, bias=0.0, scale=-0.5))
            dve(lambda v, n=n: v.scalar_tensor_tensor(
                nb_arr[:, n:n + 1], mb_arr[:, n:n + 1], -1.0, rb_arr[:, n:n + 1],
                ALU.mult, ALU.mult), waits=[a3])
    lnb_done = dve(lambda v: v.engine_nop())
    yspill0 = ("ysl0", S.val("ysl0"))
    yspill1 = ("ysl1", S.val("ysl1"))
    p4_pe = ("pe", S.val("pe"))

    if "d_yscr" in dump_t:
        gdma("gdmp", lambda g: g.dma_start(out=dump_t["d_yscr"][:, :], in_=y_scr[:, :]),
             waits=[yspill0, yspill1])
    S.checkpoint("p4")

    att_stack.close()
    xs = sb("xs", [128, NCH * DM], F32)

    p5a_stack = contextlib.ExitStack()

    def sb5(name, shape, dtype):
        return p5a_stack.enter_context(nc.sbuf_tensor(name, shape, dtype))

    xa5 = sb5("xa5", [128, DM], F32)
    ya5 = sb5("ya5", [128, DM], F32)
    lntm5 = sb5("lntm5", [128, DM], BF16)

    ln2_tile_done = [None] * NCH
    prev_x = None
    prev_y = None
    for tc in range(NCH):
        xld = sdma("x5", lambda s, tc=tc: s.dma_start(
            out=xa5[:, :], in_=xc[tc * 128:(tc + 1) * 128, :]),
            waits=[p4_pe] + ([prev_x] if prev_x else []))
        yld = sdma("y5", lambda s, tc=tc: s.dma_start(
            out=ya5[:, :], in_=y_scr[tc * 128:(tc + 1) * 128, :]),
            waits=[yspill0, yspill1] + ([prev_y] if prev_y else []))
        ayn = act(lambda s, tc=tc: s.activation(
            ya5[:, :], ya5[:, :], AF.Identity,
            bias=nb_arr[:, tc:tc + 1], scale=rb_arr[:, tc:tc + 1]),
            waits=[yld, lnb_done])
        prev_y = ayn
        dx1 = dve(lambda v, tc=tc: v.tensor_add(
            xs[:, tc * DM:(tc + 1) * DM], xa5[:, :], ya5[:, :]),
            waits=[xld, ayn])
        prev_x = dx1
        stt = ln_stats(xs[:, tc * DM:(tc + 1) * DM], [dx1])
        ap5 = act(lambda s, tc=tc: s.activation(
            lntm5[:, :], xs[:, tc * DM:(tc + 1) * DM], AF.Identity,
            bias=st_sc[:, 5:6], scale=st_sc[:, 4:5]), waits=[stt])
        for b4 in range(4):
            ln2_tile_done[tc] = transpose4(
                lambda j, b4=b4: lntm5[:, (b4 * 4 + j) * 128:(b4 * 4 + j + 1) * 128],
                [(lnx_fm, (b4 * 4 + j) * TPC + tc * 128) for j in range(4)],
                [ap5, p3_done_pe, p4_pe])
    if "d_ln2x" in dump_t:
        sdma("dmp", lambda s: s.dma_start(out=dump_t["d_ln2x"][:, :], in_=lnx_fm[:, :]),
             waits=[ln2_tile_done[NCH - 1]])
    if "d_xs1" in dump_t:
        sdma("dmp", lambda s: s.dma_start(out=dump_t["d_xs1"][:, :], in_=xs[:, :]),
             waits=[prev_x])

    p5a_pe = ("pe", S.val("pe"))
    p5a_act = ("act", S.val("act"))
    p5a_dve = ("dve", S.val("dve"))
    S.checkpoint("p5a")

    p5a_stack.close()
    p5b_stack = contextlib.ExitStack()

    def sb5b(name, shape, dtype):
        return p5b_stack.enter_context(nc.sbuf_tensor(name, shape, dtype))

    fuse = sb5b("fuse", [128, NIT * 256], BF16)
    wg_b = [sb5b(f"wg_b{i}", [128, 4096], BF16) for i in range(2)]
    wu_b = [sb5b(f"wu_b{i}", [128, 4096], BF16) for i in range(2)]
    wd_b = [sb5b(f"wd_b{i}", [128, 1024], BF16) for i in range(2)]
    sg_st = [sb5b(f"sg_st{i}", [128, 256], BF16) for i in range(2)]
    uh_st = [sb5b(f"uh_st{i}", [128, 256], BF16) for i in range(2)]
    zst = [sb5b(f"zst{i}", [128, DM], F32) for i in range(2)]
    sdown_bc = sb5b("sdown_s", [128, DM], BF16)

    p5a_barrier = [p5a_pe, p5a_act, p5a_dve]
    sdown_ld = gdma("sd5", lambda g: g.dma_start(out=sdown_bc[:, :], in_=sdown_d[:, :]),
                    waits=p5a_barrier)
    wg_last = [p5a_barrier, p5a_barrier]
    wu_last = [p5a_barrier, p5a_barrier]
    wd_last = [p5a_barrier, p5a_barrier]
    g_pool = [ps_a, ps_c]
    u_pool = [ps_b, ps_d]
    zst_free = [None, None]
    out_marks = []
    fuse_dump_mark = [None]

    for g in range(4):
        lnc_rdy = ln2_tile_done[2 * g + 1]
        lnc_rdy0 = ln2_tile_done[2 * g]
        fuse_done = None
        fuse_by_isub = [fuse_dump_mark[0], fuse_dump_mark[0]]
        gmm = umm = None
        for ib in range(32):
            bi = ib % 2
            wgl = gdma(f"wg{bi}", lambda gp, ib=ib, bi=bi: gp.dma_start(
                out=wg_b[bi].ap().rearrange("p (q m i) -> p q m i", q=4, m=4),
                in_=wg5[ib].rearrange("q f m i -> f q m i")), waits=wg_last[bi])
            wul = gdma(f"wu{bi}", lambda gp, ib=ib, bi=bi: gp.dma_start(
                out=wu_b[bi].ap().rearrange("p (q m i) -> p q m i", q=4, m=4),
                in_=wu5[ib].rearrange("q f m i -> f q m i")), waits=wu_last[bi])
            for isub in range(2):
                it = ib * 2 + isub
                g_ps = g_pool[isub]
                u_ps = u_pool[isub]
                waits = [wgl, lnc_rdy, lnc_rdy0] + war(g_ps)
                for kt in range(KT):
                    col = (kt // 4) * 1024 + (kt % 4) * 256 + isub * 128
                    gmm = pe(lambda t, col=col, kt=kt, bi=bi, g=g, g_ps=g_ps:
                             t.matmul(
                                 g_ps[:, 0:256], wg_b[bi][:, col:col + 128],
                                 lnx_fm[:, kt * TPC + g * 256:
                                        kt * TPC + (g + 1) * 256],
                                 start=(kt == 0), stop=(kt == KT - 1)),
                             waits=waits if kt == 0 else [], inc=(kt == KT - 1))
                waits = [wul, lnc_rdy, lnc_rdy0] + war(u_ps)
                for kt in range(KT):
                    col = (kt // 4) * 1024 + (kt % 4) * 256 + isub * 128
                    umm = pe(lambda t, col=col, kt=kt, bi=bi, g=g, u_ps=u_ps:
                             t.matmul(
                                 u_ps[:, 0:256], wu_b[bi][:, col:col + 128],
                                 lnx_fm[:, kt * TPC + g * 256:
                                        kt * TPC + (g + 1) * 256],
                                 start=(kt == 0), stop=(kt == KT - 1)),
                             waits=waits if kt == 0 else [], inc=(kt == KT - 1))
                warf = [fuse_by_isub[isub]] if fuse_by_isub[isub] else []
                asg = act(lambda s, it=it, isub=isub, g_ps=g_ps: s.activation(
                    sg_st[isub][:, :], g_ps[:, 0:256], AF.Silu,
                    bias=0.0, scale=sg_t[:, it:it + 1]),
                    waits=[gmm, consts_s] + warf)
                set_war(g_ps, asg)
                auh = act(lambda s, it=it, isub=isub, u_ps=u_ps: s.activation(
                    uh_st[isub][:, :], u_ps[:, 0:256], AF.Identity,
                    bias=0.0, scale=su_t[:, it:it + 1]), waits=[umm] + warf)
                set_war(u_ps, auh)
                fuse_done = dve(lambda v, it=it, isub=isub: v.tensor_mul(
                    fuse[:, it * 256:(it + 1) * 256],
                    sg_st[isub][:, :], uh_st[isub][:, :]), waits=[asg, auh])
                fuse_by_isub[isub] = fuse_done
            wg_last[bi] = [gmm]
            wu_last[bi] = [umm]
        if g == 0 and "d_fuse" in dump_t:
            fdm = sdma("dmp", lambda s: s.dma_start(out=dump_t["d_fuse"][:, :],
                                                    in_=fuse[:, :]),
                       waits=[fuse_done])
            fuse_dump_mark[0] = fdm
        for oc in range(4):
            dps = {0: [ps_a, ps_c][oc % 2], 1: [ps_b, ps_d][oc % 2]}
            dmm = {0: None, 1: None}
            for itp in range(32):
                bi = itp % 2
                wdl = gdma(f"wd{bi}", lambda gp, oc=oc, itp=itp, bi=bi: gp.dma_start(
                    out=wd_b[bi].ap().rearrange("p (m o) -> p m o", m=2),
                    in_=wd5[oc, itp]), waits=wd_last[bi])
                for itm in range(2):
                    it = itp * 2 + itm
                    for tsub in range(2):
                        ps = dps[tsub]
                        waits = []
                        if itm == 0 and tsub == 0:
                            waits.append(wdl)
                        if it == 0:
                            waits.append(fuse_done)
                            waits += war(ps)
                        dmm[tsub] = pe(
                            lambda t, ps=ps, it=it, itm=itm, tsub=tsub, bi=bi:
                            t.matmul(
                                ps[:, 0:512],
                                fuse[:, it * 256 + tsub * 128:
                                     it * 256 + tsub * 128 + 128],
                                wd_b[bi][:, itm * 512:(itm + 1) * 512],
                                start=(it == 0), stop=(it == NIT - 1)),
                            waits=waits,
                            inc=(itm == 1 and tsub == 1) or (it == NIT - 1))
                wd_last[bi] = [dmm[1]]
            for tsub in range(2):
                ps = dps[tsub]
                waits = [dmm[tsub], sdown_ld]
                if zst_free[tsub] is not None:
                    waits.append(zst_free[tsub])
                dz = dve(lambda v, ps=ps, oc=oc, tsub=tsub: v.tensor_mul(
                    zst[tsub][:, oc * 512:(oc + 1) * 512],
                    ps[:, 0:512], sdown_bc[:, oc * 512:(oc + 1) * 512]),
                    waits=waits)
                set_war(ps, dz)
                act(lambda s, oc=oc, tsub=tsub: s.activation(
                    junk_a[:, 0:512], zst[tsub][:, oc * 512:(oc + 1) * 512],
                    AF.Identity, bias=0.0, scale=1.0,
                    accum_out=sp1[:, tsub * 4 + oc:tsub * 4 + oc + 1]),
                    waits=[dz])
                dve(lambda v, oc=oc, tsub=tsub: v.scalar_tensor_tensor(
                    junk_s[:, 0:512], zst[tsub][:, oc * 512:(oc + 1) * 512], 1.0,
                    zst[tsub][:, oc * 512:(oc + 1) * 512],
                    ALU.mult, ALU.mult,
                    accum_out=sp2[:, tsub * 4 + oc:tsub * 4 + oc + 1]),
                    waits=[dz])
        zdump = []
        if g == 0 and "d_z" in dump_t:
            zw = [("act", S.val("act")), ("dve", S.val("dve"))]
            for ts_ in range(2):
                zdump.append(sdma("dmp2", lambda s, ts_=ts_: s.dma_start(
                    out=dump_t["d_z"][:, ts_ * DM:(ts_ + 1) * DM],
                    in_=zst[ts_][:, :]), waits=zw))
        stats_in = [("act", S.val("act")), ("dve", S.val("dve"))]
        for tsub in range(2):
            tc = g * 2 + tsub
            dve(lambda v, tsub=tsub: v.reduce_sum(
                st_sc[:, 0:1], sp1[:, tsub * 4:tsub * 4 + 4], axis=AX.X),
                waits=stats_in)
            dve(lambda v, tsub=tsub: v.reduce_sum(
                st_sc[:, 1:2], sp2[:, tsub * 4:tsub * 4 + 4], axis=AX.X))
            dve(lambda v: v.tensor_scalar_mul(st_sc[:, 2:3], st_sc[:, 0:1],
                                              1.0 / DM))
            dve(lambda v: v.tensor_mul(st_sc[:, 6:7], st_sc[:, 2:3],
                                       st_sc[:, 2:3]))
            d4v = dve(lambda v: v.scalar_tensor_tensor(
                st_sc[:, 3:4], st_sc[:, 1:2], 1.0 / DM, st_sc[:, 6:7],
                ALU.mult, ALU.subtract))
            d4v = dve(lambda v: v.tensor_scalar_add(st_sc[:, 3:4], st_sc[:, 3:4],
                                                    EPS_LN))
            act(lambda s: s.activation(st_sc[:, 7:8], st_sc[:, 3:4], AF.Ln,
                                       bias=0.0, scale=1.0), waits=[d4v])
            a3 = act(lambda s: s.activation(st_sc[:, 4:5], st_sc[:, 7:8],
                                            AF.Exp, bias=0.0, scale=-0.5))
            d5v = dve(lambda v: v.scalar_tensor_tensor(
                st_sc[:, 5:6], st_sc[:, 2:3], -1.0, st_sc[:, 4:5],
                ALU.mult, ALU.mult), waits=[a3])
            azn = act(lambda s, tsub=tsub: s.activation(
                zst[tsub][:, :], zst[tsub][:, :], AF.Identity,
                bias=st_sc[:, 5:6], scale=st_sc[:, 4:5]),
                waits=[d5v] + (zdump[-1:] if zdump else []))
            dxf = dve(lambda v, tc=tc, tsub=tsub: v.tensor_add(
                xs[:, tc * DM:(tc + 1) * DM], xs[:, tc * DM:(tc + 1) * DM],
                zst[tsub][:, :]), waits=[azn])
            stm = sdma("out", lambda s, tc=tc: s.dma_start(
                out=yout[tc * 128:(tc + 1) * 128, :],
                in_=xs[:, tc * DM:(tc + 1) * DM]), waits=[dxf])
            zst_free[tsub] = dxf
            out_marks.append(stm)

    if stop_after is not None:
        lens, cnts = S.marks[stop_after]
        for e in ENGINES:
            S.recs[e] = S.recs[e][:lens[e]]
        S.cnt = cnts
    for sem in SY_DMA_SEMS:
        v = S.val(sem)
        if v > 0:
            S.emit("sy", lambda s, sem=sem, v=v: s.wait_ge(sems[sem], v))
    for sem in GP_DMA_SEMS:
        v = S.val(sem)
        if v > 0:
            S.emit("gp", lambda g, sem=sem, v=v: g.wait_ge(sems[sem], v))

    serialized = ("dve", "act")

    def replay(ename, eng):
        last_wait = {}
        nprev = 0
        for fn, waits, inc_sem, amt in S.recs[ename]:
            if ename in serialized and nprev > 0:
                eng.wait_ge(sems[ename], nprev)
            for (sname, v) in waits:
                if sname == ename and sname in COMPUTE_SEMS:
                    continue
                if last_wait.get(sname, -1) >= v:
                    continue
                eng.wait_ge(sems[sname], v)
                last_wait[sname] = v
            inst = fn(eng)
            if inc_sem is not None:
                if amt is None:
                    inst.then_inc(sems[inc_sem])
                else:
                    inst.then_inc(sems[inc_sem], amt)
            if inc_sem == ename:
                nprev += 1

    with nc.Block() as block:
        @block.tensor
        def _(eng):
            replay("pe", eng)

        @block.scalar
        def _(eng):
            replay("act", eng)

        @block.vector
        def _(eng):
            replay("dve", eng)

        @block.gpsimd
        def _(eng):
            replay("gp", eng)

        @block.sync
        def _(eng):
            replay("sy", eng)

    p5b_stack.close()
    stack.close()
    return nc


_NC = None
_NC_KEY = None


def _get_program(dumps=()):
    global _NC, _NC_KEY
    key = tuple(sorted(dumps))
    if _NC is None or _NC_KEY != key:
        _NC = build_program(dumps=key)
        _NC_KEY = key
    return _NC


def _prep_shared(inputs):
    d = {}
    for p, name in [("q", "L0_attn_Wq"), ("k", "L0_attn_Wk"),
                    ("v", "L0_attn_Wv"), ("o", "L0_attn_Wo")]:
        W = np.asarray(inputs[name])
        d[f"w4{p}"] = np.ascontiguousarray(
            W.reshape(KT, 128, 8, 256).transpose(2, 0, 1, 3))
    d["wg5"] = np.ascontiguousarray(
        np.asarray(inputs["L0_ffn_w_gate"]).reshape(4, 4, 128, 32, 256)
        .transpose(3, 0, 2, 1, 4))
    d["wu5"] = np.ascontiguousarray(
        np.asarray(inputs["L0_ffn_w_up"]).reshape(4, 4, 128, 32, 256)
        .transpose(3, 0, 2, 1, 4))
    d["wd5"] = np.ascontiguousarray(
        np.asarray(inputs["L0_ffn_w_down"]).reshape(32, 2, 128, 4, 512)
        .transpose(3, 0, 2, 1, 4))
    d["bqkv"] = np.ascontiguousarray(np.stack(
        [np.asarray(inputs["L0_attn_bq"]).reshape(KT, 128),
         np.asarray(inputs["L0_attn_bk"]).reshape(KT, 128),
         np.asarray(inputs["L0_attn_bv"]).reshape(KT, 128)],
        axis=0).reshape(48, 128).T)
    d["bo_bc"] = np.ascontiguousarray(
        np.broadcast_to(np.asarray(inputs["L0_attn_bo"]), (128, DM)))
    d["sdown_bc"] = np.ascontiguousarray(
        np.broadcast_to(np.asarray(inputs["L0_ffn_s_down"]), (128, DM)))
    d["sg_t"] = np.ascontiguousarray(
        np.asarray(inputs["L0_ffn_s_gate"]).reshape(NIT, 128).T)
    d["su_t"] = np.ascontiguousarray(
        np.asarray(inputs["L0_ffn_s_up"]).reshape(NIT, 128).T)
    d["id128"] = np.eye(128, dtype=np.float32)
    d["triu"] = np.triu(np.ones((128, 128), dtype=np.float32))
    d["triu4d"] = np.ascontiguousarray(np.tile(d["triu"], (1, 4)))
    return d


def make_in_maps(inputs):
    shared = _prep_shared(inputs)
    x = np.asarray(inputs["L0_attn_x"])
    in_maps = []
    for c in range(8):
        b, qi = c // 4, c % 4
        m = dict(shared)
        m["xc"] = np.ascontiguousarray(x[b, qi * TPC:(qi + 1) * TPC, :])
        pm = np.zeros((128, 4), dtype=np.float32)
        pm[:, :qi] = 1.0
        m["pmask"] = pm
        in_maps.append(m)
    return in_maps


def kernel(**inputs):
    nc = _get_program()
    in_maps = make_in_maps(inputs)
    res = run_bass_kernel_spmd(nc, in_maps, core_ids=list(range(8)))
    out = np.empty((2, 4096, DM), dtype=np.float32)
    for c in range(8):
        b, qi = c // 4, c % 4
        out[b, qi * TPC:(qi + 1) * TPC, :] = res.results[c]["yout"]
    return out


if __name__ == "__main__":
    import sys
    nc = build_program(dumps=tuple(sys.argv[1:]))
    print("build ok")
